# version 06
# baseline (speedup 1.0000x reference)
import numpy as np
import ml_dtypes
import concourse.bass as bass
import concourse.mybir as mybir
from concourse.bass_utils import run_bass_kernel_spmd

F32 = mybir.dt.float32
BF16 = mybir.dt.bfloat16
AF = mybir.ActivationFunctionType
ALU = mybir.AluOpType
AX = mybir.AxisListType

T, D, I = 4096, 1024, 3072
NDC, NIC, NHC, NEC = 8, 24, 8, 8
NTC = 8
C_RNE = 12582912.0
QMAX = 32767.0
FQ = 65536.0
EPS = 1e-5
N_STAGES = 3


class Prog:

    ENGS = ("sync", "vector", "scalar", "tensor", "gpsimd")
    NDSEM = 6

    def __init__(self):
        self.q = {e: [] for e in self.ENGS}
        self.n = {e: 0 for e in self.ENGS}
        self.di = {e: 0 for e in self.ENGS}
        self.dc = {e: [0] * self.NDSEM for e in self.ENGS}
        self.cc = 0

    @staticmethod
    def _flat(waits):
        out = []
        for w in waits:
            if w is None:
                continue
            if isinstance(w, list):
                out.extend(x for x in w if x is not None)
            else:
                out.append(w)
        return out

    def op(self, eng, f, waits=(), guard=True, inc=True):
        prev = self.n[eng] if guard else 0
        if inc:
            self.n[eng] += 1
        cnt = self.n[eng]
        ws = self._flat(waits)

        def emit(engine, sems):
            for (key, val) in ws:
                if val > 0:
                    engine.wait_ge(sems[key], val)
            if prev > 0:
                engine.wait_ge(sems["op" + eng], prev)
            ins = f(engine)
            if inc:
                ins.then_inc(sems["op" + eng], 1)
        self.q[eng].append(emit)
        return ("op" + eng, cnt)

    def dma_op(self, eng, out, in_, waits=()):
        i = self.di[eng] % self.NDSEM
        self.di[eng] += 1
        self.dc[eng][i] += 16
        cnt = self.dc[eng][i]
        key = "d" + eng + str(i)
        ws = self._flat(waits)

        def emit(engine, sems):
            for (k2, val) in ws:
                if val > 0:
                    engine.wait_ge(sems[k2], val)
            if cnt > 16:
                engine.wait_ge(sems[key], cnt - 16)
            engine.dma_start(out=out, in_=in_).then_inc(sems[key], 16)
        self.q[eng].append(emit)
        return (key, cnt)

    def cc_op(self, ins_ap, outs_ap, waits=()):
        self.cc += 1
        cnt = self.cc
        ws = self._flat(waits)

        def emit(engine, sems):
            for (key, val) in ws:
                if val > 0:
                    engine.wait_ge(sems[key], val)
            engine.collective_compute(
                "AllReduce", ALU.max, replica_groups=[list(range(8))],
                ins=[ins_ap], outs=[outs_ap]).then_inc(sems["cc"], 1)
        self.q["gpsimd"].append(emit)
        return ("cc", cnt)

    def wv(self, eng):
        return ("op" + eng, self.n[eng])

    def wd(self, eng):
        return [("d" + eng + str(i), self.dc[eng][i])
                for i in range(self.NDSEM) if self.dc[eng][i] > 0]


def build_program(n_stages=N_STAGES, debug=False):
    nc = bass.Bass("TRN2", target_bir_lowering=False, debug=False)
    p = Prog()

    x_in = nc.dram_tensor("x_in", [T, D], F32, kind="ExternalInput").ap()
    y_out = nc.dram_tensor("y", [T, D], F32, kind="ExternalOutput").ap()
    wu_in = nc.dram_tensor("w_up", [I, D], F32, kind="ExternalInput").ap()
    wg_in = nc.dram_tensor("w_gate", [I, D], F32, kind="ExternalInput").ap()
    wd_in = nc.dram_tensor("w_down", [D, I], F32, kind="ExternalInput").ap()
    wdec_in = nc.dram_tensor("w_dec", [D, D], F32, kind="ExternalInput").ap()
    winj_in = nc.dram_tensor("w_inj", [D, D], F32, kind="ExternalInput").ap()
    bdec_in = nc.dram_tensor("b_dec", [128, NEC], F32, kind="ExternalInput").ap()
    binj_in = nc.dram_tensor("b_inj", [128, NEC], F32, kind="ExternalInput").ap()
    ones_in = nc.dram_tensor("cones", [128, 128], BF16, kind="ExternalInput").ap()
    idf_in = nc.dram_tensor("cidf", [128, 128], F32, kind="ExternalInput").ap()
    idb_in = nc.dram_tensor("cidb", [128, 128], BF16, kind="ExternalInput").ap()
    cst_in = nc.dram_tensor("ccst", [128, 8], F32, kind="ExternalInput").ap()

    dram = lambda name, shape, dt=F32: nc.dram_tensor(name, shape, dt)
    ddram = (lambda name, shape, dt=F32: nc.dram_tensor(name, shape, dt,
             kind="ExternalOutput")) if debug else dram
    sp_xln = ddram("sp_xln", [NTC, NDC, 128, 512], BF16)
    sp_up = ddram("sp_up", [NIC, NTC, 128, 512], BF16)
    sp_gate = ddram("sp_gate", [NIC, NTC, 128, 512], BF16)
    sp_P = ddram("sp_P", [NIC, NTC, 128, 512], BF16)
    sp_P2 = ddram("sp_P2", [NTC, NIC, 128, 512], BF16)
    sp_xbf = dram("sp_xbf", [NTC, NDC, 128, 512], BF16)
    sp_xq = dram("sp_xq", [4, 128, T])
    wqT_u = dram("wqT_u", [NIC, NDC, 128, 128], BF16)
    wqT_g = dram("wqT_g", [NIC, NDC, 128, 128], BF16)
    wqT_d = dram("wqT_d", [NHC, NIC, 128, 128], BF16)
    wT_dec = dram("wT_dec", [NEC, NDC, 128, 128], BF16)
    wT_inj = dram("wT_inj", [NEC, NDC, 128, 128], BF16)
    dbg_out = (nc.dram_tensor("dbg", [128, 16], F32, kind="ExternalOutput").ap()
               if debug else None)
    ncc = 16
    cc_i = [dram(f"cc_i{k}", [1, 4]) for k in range(ncc)]
    cc_o = [dram(f"cc_o{k}", [1, 4]) for k in range(ncc)]

    from contextlib import ExitStack
    stack = ExitStack()
    sb = lambda name, shape, dt=F32: stack.enter_context(nc.sbuf_tensor(name, shape, dt))
    xs = [sb(f"x{dc}", [128, T]) for dc in range(NDC)]
    ones = sb("ones", [128, 128], BF16)
    idf = sb("idf", [128, 128], F32)
    idb = sb("idb", [128, 128], BF16)
    bdec = sb("bdec", [128, NEC])
    binj = sb("binj", [128, NEC])
    csdn = sb("csdn", [128, NEC])
    csin = sb("csin", [128, NEC])
    tev = sb("tev", [128, 8, 128], BF16)
    tmax = sb("tmax", [128, 8])
    runm = sb("runm", [128, 4])
    scal = sb("scal", [128, 16])
    mt = sb("mt", [128, 512])
    rt = sb("rt", [128, 512])
    vt = sb("vt", [128, 512])
    xlb = sb("xlb", [128, 2, 512])
    xlc = sb("xlc", [128, 2, 512], BF16)
    xlbB = lambda s: xlc[:, s, :]
    xbfb = sb("xbfb", [128, 2, 512], BF16)
    xsqb = sb("xsqb", [128, 2, 512], BF16)
    carr = sb("carr", [128, NEC])
    mx1 = sb("mx1", [128, 1])
    mxc = sb("mxc", [4, 1])
    cst = sb("cst", [128, 8])
    AR = 15360
    arena = sb("arena", [128, AR])

    def carve(off, n):
        assert off + n <= AR, (off, n)
        return arena[:, off:off + n], off + n

    o = 0
    stage, o = carve(o, 2048)
    stg = stage.rearrange("p (s n) -> p s n", n=1024)
    wdq, o = carve(o, 1536)
    wdqb_r, o = carve(o, 768)
    wdqb = wdqb_r.bitcast(BF16)
    srep_r, o = carve(o, 1536)
    rrep_r, o = carve(o, 1536)
    srep = srep_r.rearrange("p (b k) -> p b k", k=32)
    rrep = rrep_r.rearrange("p (b k) -> p b k", k=32)
    mxs, o = carve(o, 48)
    sss, o = carve(o, 48)
    rrr, o = carve(o, 48)

    o = 0
    xg_r, o = carve(o, 2048)
    xgv = xg_r.bitcast(BF16).rearrange("p (d n) -> p d n", n=512)
    evb_r, o = carve(o, 1024)
    evb = evb_r.bitcast(BF16).rearrange("p (s n) -> p s n", n=512)
    wls_r, o = carve(o, 2048)
    wls = wls_r.bitcast(BF16).rearrange("p (s d n) -> p s d n", s=4, d=NDC)

    o = 0
    gld_r, o = carve(o, 2048)
    gld = gld_r.bitcast(BF16).rearrange("p (s n) -> p s n", n=2048)
    gwk_r, o = carve(o, 4096)
    gwk = gwk_r.rearrange("p (s n) -> p s n", n=2048)
    kbf_r, o = carve(o, 4096)
    kbf = kbf_r.rearrange("p (s n) -> p s n", n=2048)
    uld_r, o = carve(o, 2048)
    uld = uld_r.bitcast(BF16).rearrange("p (s n) -> p s n", n=2048)

    o = 0
    kob_r, o = carve(o, 12288)
    kob = kob_r.bitcast(BF16).rearrange("p (i n) -> p i n", n=1024)
    wdh_r, o = carve(o, 3072)
    wdh = wdh_r.bitcast(BF16).rearrange("p (s i n) -> p s i n", s=4, i=12)

    o = 0
    r_q, o = carve(o, 1024)
    rm_q, o = carve(o, 1024)
    dpre, o = carve(o, 1024)
    ipre, o = carve(o, 1024)
    xkb_r, o = carve(o, 4096)
    xkb = xkb_r.bitcast(BF16).rearrange("p (d n) -> p d n", n=1024)
    wsl_r, o = carve(o, 1024)
    wsl = wsl_r.bitcast(BF16).rearrange("p (w d n) -> p w d n", w=2, d=NDC)

    obuf = arena[:, 0:2048].rearrange("p (s n) -> p s n", n=1024)

    psum = stack.enter_context(nc.psum_tensor("psum", [128, 6, 512], F32))
    psT = stack.enter_context(nc.psum_tensor("psT", [128, 512], F32))
    psB = stack.enter_context(nc.psum_tensor("psB", [128, 1024], BF16))

    p.dma_op("sync", ones[:, :], ones_in)
    p.dma_op("sync", idf[:, :], idf_in)
    p.dma_op("sync", idb[:, :], idb_in)
    p.dma_op("sync", bdec[:, :], bdec_in)
    d_c2 = p.dma_op("sync", binj[:, :], binj_in)
    d_c2 = p.dma_op("sync", cst[:, :], cst_in)

    in_state = {"ev": [None, None], "ps": None}

    def in_unit(tc):
        slot = tc % 2
        d_st = p.dma_op("sync", stg[:, slot, :], x_in[tc * 128:(tc + 1) * 128, :],
                        waits=[in_state["ev"][slot], d_c2 if tc == 0 else None])
        for dc in range(NDC):
            ps = (tc * NDC + dc) % 4
            tv = p.op("tensor", lambda e, sl=slot, d2=dc, ps2=ps: e.transpose(
                psT[:, ps2 * 128:(ps2 + 1) * 128], stg[:, sl, d2 * 128:(d2 + 1) * 128],
                idf[:, :]),
                waits=[d_st, in_state["ps"]])
            in_state["ps"] = p.op("scalar", lambda e, d2=dc, t2=tc, ps2=ps: e.copy(
                xs[d2][:, t2 * 128:(t2 + 1) * 128], psT[:, ps2 * 128:(ps2 + 1) * 128]),
                waits=[tv])
        in_state["ev"][slot] = in_state["ps"]

    def dequant_tile(src_ap, ncols):
        nblk = ncols // 32
        d_w = p.dma_op("sync", wdq[:, 0:ncols], src_ap,
                       waits=[p.wv("vector"), p.wv("tensor")])
        v3 = wdq[:, 0:ncols].rearrange("p (b k) -> p b k", k=32)
        p.op("vector", lambda e: e.tensor_reduce(mxs[:, 0:nblk], v3, AX.X, ALU.max,
                                                 apply_absolute_value=True),
             waits=[d_w])
        p.op("vector", lambda e: e.tensor_scalar(sss[:, 0:nblk], mxs[:, 0:nblk],
                                                 1.0 / 7.0, 1e-12, ALU.mult, ALU.max))
        p.op("vector", lambda e: e.reciprocal(rrr[:, 0:nblk], sss[:, 0:nblk]))
        p.op("vector", lambda e: e.tensor_copy(srep[:, 0:nblk, 0:1], sss[:, 0:nblk, None]))
        p.op("vector", lambda e: e.tensor_copy(rrep[:, 0:nblk, 0:1], rrr[:, 0:nblk, None]))
        w = 1
        while w < 32:
            p.op("vector", lambda e, w2=w: e.tensor_copy(srep[:, 0:nblk, w2:2 * w2],
                                                         srep[:, 0:nblk, 0:w2]))
            p.op("vector", lambda e, w2=w: e.tensor_copy(rrep[:, 0:nblk, w2:2 * w2],
                                                         rrep[:, 0:nblk, 0:w2]))
            w *= 2
        p.op("vector", lambda e: e.tensor_tensor(v3, v3, rrep[:, 0:nblk, :], ALU.mult))
        p.op("vector", lambda e: e.tensor_scalar(wdq[:, 0:ncols], wdq[:, 0:ncols],
                                                 C_RNE, -C_RNE, ALU.add, ALU.add))
        p.op("vector", lambda e: e.tensor_tensor(v3, v3, srep[:, 0:nblk, :], ALU.mult))
        return p.op("vector", lambda e: e.tensor_copy(wdqb[:, 0:ncols], wdq[:, 0:ncols]),
                    waits=[tr_state["lastT"]])

    tr_state = {"psB": None, "tev": None, "lastT": None}

    def transpose_out(ncols, dst_gfn, v_ready):
        nch = ncols // 128
        j = 0
        while j < nch:
            g = min(8, nch - j)
            tvs = None
            for q in range(g):
                tvs = p.op("tensor", lambda e, j2=j + q, q2=q: e.transpose(
                    psB[:, q2 * 128:(q2 + 1) * 128], wdqb[:, j2 * 128:(j2 + 1) * 128],
                    idb[:, :]),
                    waits=[v_ready, tr_state["psB"]] if q == 0 else [],
                    guard=False, inc=(q == g - 1))
            sv = p.op("scalar", lambda e, g2=g: e.copy(
                tev[:, 0:g2, :], psB[:, 0:g2 * 128]),
                waits=[tvs, tr_state["tev"]])
            tr_state["psB"] = sv
            tr_state["tev"] = p.dma_op("gpsimd", dst_gfn(j, g), tev[:, 0:g, :],
                                       waits=[sv])
            j += g
        tr_state["lastT"] = tvs

    dq_units = []

    def _dq_ug(w_src, dst, ic):
        v_r = dequant_tile(w_src[ic * 128:(ic + 1) * 128, :], D)
        transpose_out(D, lambda j, g, ic2=ic, d2=dst:
                      d2.ap()[ic2, j:j + g].rearrange("c p n -> p c n"), v_r)

    def _dq_dn(hc, ih):
        v_r = dequant_tile(wd_in[hc * 128:(hc + 1) * 128, ih * 1536:(ih + 1) * 1536], 1536)
        transpose_out(1536, lambda j, g, h2=hc, i0=12 * ih:
                      wqT_d.ap()[h2, i0 + j:i0 + j + g].rearrange("c p n -> p c n"), v_r)

    def _dq_ssm(w_src, dstT, cs_t, ec):
        d_w = p.dma_op("sync", wdq[:, 0:D], w_src[ec * 128:(ec + 1) * 128, :],
                       waits=[p.wv("vector"), p.wv("tensor")])
        vb = p.op("vector", lambda e: e.tensor_copy(wdqb[:, 0:D], wdq[:, 0:D]),
                  waits=[d_w, tr_state["lastT"]])
        p.op("vector", lambda e, ec2=ec, c2=cs_t: e.tensor_reduce(
            c2[:, ec2:ec2 + 1], wdqb[:, 0:D], AX.X, ALU.add))
        p.op("vector", lambda e, ec2=ec, c2=cs_t: e.tensor_scalar(
            c2[:, ec2:ec2 + 1], c2[:, ec2:ec2 + 1], -1.0, None, ALU.mult))
        transpose_out(D, lambda j, g, e2=ec, d2=dstT:
                      d2.ap()[e2, j:j + g].rearrange("c p n -> p c n"), vb)

    for (w_src, dst) in ((wu_in, wqT_u), (wg_in, wqT_g)):
        for ic in range(NIC):
            dq_units.append(lambda w2=w_src, d2=dst, i2=ic: _dq_ug(w2, d2, i2))
    for hc in range(NHC):
        for ih in range(2):
            dq_units.append(lambda h2=hc, i2=ih: _dq_dn(h2, i2))
    for (w_src, dstT, cs_t) in ((wdec_in, wT_dec, csdn), (winj_in, wT_inj, csin)):
        for ec in range(NEC):
            dq_units.append(lambda w2=w_src, d2=dstT, c2=cs_t, e2=ec: _dq_ssm(w2, d2, c2, e2))

    bno = [0]

    def allreduce_maxes(m, src_col, dst_col):
        k = bno[0]
        bno[0] += 1
        tv = p.op("tensor", lambda e: e.transpose(psT[0:m, 0:128],
                                                  runm[:, src_col:src_col + m], idf[:, :]),
                  waits=[p.wv("vector"), p.wv("scalar"), p.wv("gpsimd")])
        vv = p.op("vector", lambda e: e.tensor_reduce(mxc[0:m, 0:1], psT[0:m, 0:128],
                                                      AX.X, ALU.max),
                  waits=[tv])
        d_up = p.dma_op("sync", cc_i[k].ap()[0:1, 0:m].rearrange("o m -> m o"),
                        mxc[0:m, 0:1], waits=[vv])
        cv = p.cc_op(cc_i[k].ap()[0:1, 0:m], cc_o[k].ap()[0:1, 0:m],
                     waits=[d_up])
        return p.dma_op("sync", scal[:, dst_col:dst_col + m],
                        cc_o[k].ap()[0:1, 0:m].broadcast_to([128, m]),
                        waits=[cv])

    def ln_stats_tc(tc, spill_xbf, r_dst):
        tv = None
        mmx = [0] * NDC
        mmx2 = [0] * NDC
        for dc in range(NDC):
            sl = dc % 2
            vv = p.op("vector", lambda e, d2=dc, s2=sl: e.tensor_copy(
                xbfb[:, s2, :], xs[d2][:, tc * 512:(tc + 1) * 512]),
                waits=[mmx[dc - 2] if dc >= 2 else p.wv("tensor"),
                       p.wv("scalar"),
                       p.wd("gpsimd") if spill_xbf else None])
            sv = p.op("scalar", lambda e, d2=dc, s2=sl: e.activation(
                xsqb[:, s2, :], xs[d2][:, tc * 512:(tc + 1) * 512], AF.Square),
                waits=[mmx2[dc - 2] if dc >= 2 else p.wv("tensor")])
            mmx[dc] = p.op("tensor", lambda e, s2=sl, d2=dc: e.matmul(
                psum[:, 0, :], ones[:, :], xbfb[:, s2, :],
                start=(d2 == 0), stop=(d2 == NDC - 1)),
                waits=[vv], guard=False)
            tv = mmx2[dc] = p.op("tensor", lambda e, s2=sl, d2=dc: e.matmul(
                psum[:, 1, :], ones[:, :], xsqb[:, s2, :],
                start=(d2 == 0), stop=(d2 == NDC - 1)),
                waits=[sv], guard=False)
            if spill_xbf:
                p.dma_op("gpsimd", sp_xbf[tc, dc], xbfb[:, sl, :],
                         waits=[vv])
        p.op("vector", lambda e: e.tensor_scalar(mt[:, :], psum[:, 0, :], 1.0 / D, None,
                                                 ALU.mult), waits=[tv])
        p.op("vector", lambda e: e.tensor_scalar(vt[:, :], psum[:, 1, :], 1.0 / D, None,
                                                 ALU.mult))
        p.op("vector", lambda e: e.scalar_tensor_tensor(rt[:, :], mt[:, :], -1.0, mt[:, :],
                                                        ALU.mult, ALU.mult))
        vv = p.op("vector", lambda e: e.tensor_tensor(vt[:, :], vt[:, :], rt[:, :], ALU.add))
        p.op("scalar", lambda e: e.activation(vt[:, :], vt[:, :], AF.Ln, bias=cst[:, 0:1]),
             waits=[vv])
        return p.op("scalar", lambda e, rd=r_dst: e.activation(rd, vt[:, :], AF.Exp,
                                                               scale=cst[:, 1:2]))

    def ln1_make():
        st = {"tok": [None, None]}
        p.op("vector", lambda e: e.tensor_scalar(runm[:, 0:4], runm[:, 0:4], 0.0, None,
                                                 ALU.mult))

        def unit(tc):
            sv = ln_stats_tc(tc, False, rt[:, :])
            for dc in range(NDC):
                sl = dc % 2
                p.op("vector", lambda e, d2=dc, s2=sl, t3=tc: e.tensor_tensor(
                    xlb[:, s2, :], xs[d2][:, t3 * 512:(t3 + 1) * 512], mt[:, :],
                    ALU.subtract),
                    waits=[sv, st["tok"][sl]])
                vv = p.op("vector", lambda e, s2=sl: e.tensor_tensor(
                    xlbB(s2), xlb[:, s2, :], rt[:, :], ALU.mult))
                p.op("vector", lambda e, s2=sl, d2=dc: e.tensor_reduce(
                    tmax[:, d2:d2 + 1], xlbB(s2), AX.X, ALU.max,
                    apply_absolute_value=True))
                st["tok"][sl] = p.dma_op("gpsimd", sp_xln[tc, dc], xlbB(sl),
                                         waits=[vv])
            p.op("vector", lambda e: e.tensor_reduce(mx1[:, :], tmax[:, 0:NDC], AX.X,
                                                     ALU.max))
            p.op("vector", lambda e: e.tensor_tensor(runm[:, 0:1], runm[:, 0:1],
                                                     mx1[:, :], ALU.max))

        def finish():
            vxw = p.wv("vector")
            sxw = p.wv("scalar")
            xsp = [p.dma_op("gpsimd", sp_xq[k], xs[4 + k][:, :], waits=[vxw, sxw])
                   for k in range(4)]
            return {"xsp": xsp, "g_ln": p.wd("gpsimd")}
        return unit, finish

    def emit_ffn(pre=None):
        qxf = [xs[4 + dc // 2][:, :].bitcast(BF16)[:, (dc % 2) * T:(dc % 2 + 1) * T]
               for dc in range(NDC)]
        if pre is None:
            u_ln, f_ln = ln1_make()
            for tc in range(NTC):
                u_ln(tc)
            pre = f_ln()
        xsp = pre["xsp"]
        g_ln = pre["g_ln"]
        d_sc = allreduce_maxes(1, 0, 14)
        p.op("vector", lambda e: e.tensor_scalar(scal[:, 0:1], scal[:, 14:15], 1.0 / QMAX,
                                                 1e-12, ALU.mult, ALU.max),
             waits=[d_sc])
        p.op("vector", lambda e: e.reciprocal(scal[:, 1:2], scal[:, 0:1]))
        vz = p.op("vector", lambda e: e.tensor_scalar(runm[:, 0:2], runm[:, 0:2], 0.0,
                                                      None, ALU.mult))
        ts1tok = None
        a2tok = [None, None]
        qxtok = [None] * NTC
        for tc in range(NTC):
            d_x = p.dma_op("sync", xgv[:, :, :], sp_xln[tc].rearrange("d p n -> p d n"),
                           waits=[g_ln, ts1tok] +
                                 ([p.wv("vector"), p.wv("tensor")] if tc == 0 else []))
            for dc in range(NDC):
                sl = dc % 2
                tv1 = p.op("vector", lambda e, d2=dc, s2=sl: e.tensor_scalar(
                    xlb[:, s2, :], xgv[:, d2, :], scal[:, 1:2], C_RNE, ALU.mult, ALU.add),
                    waits=[d_x, vz, a2tok[sl]])
                a2tok[sl] = p.op("scalar", lambda e, d2=dc, s2=sl, t3=tc: e.activation(
                    qxf[d2][:, t3 * 512:(t3 + 1) * 512], xlb[:, s2, :], AF.Identity,
                    bias=cst[:, 3:4]),
                    waits=[tv1, xsp[dc // 2]])
            ts1tok = tv1
            qxtok[tc] = a2tok[1]
        evtok = {0: None, 1: None, 2: None, 3: None}
        sptok = [None, None]
        mictok = [None] * NIC
        for ic in range(NIC):
            ws0, ws1 = (2 * ic) % 4, (2 * ic + 1) % 4
            d_wu = p.dma_op("sync", wls[:, ws0, :, :],
                            wqT_u[ic].rearrange("c p n -> p c n"),
                            waits=[mictok[ic - 2] if ic >= 2 else None,
                                   init_g if ic == 0 else None,
                                   p.wv("vector") if ic == 0 else None])
            d_wg = p.dma_op("sync", wls[:, ws1, :, :],
                            wqT_g[ic].rearrange("c p n -> p c n"))
            gtok = None
            for tc in range(NTC):
                for (ui, d_w, ws2, spd) in ((0, d_wu, ws0, sp_up), (1, d_wg, ws1, sp_gate)):
                    bank = 2 * (tc % 2) + ui
                    for dc in range(NDC):
                        gtok = p.op("tensor", lambda e, w2=ws2, d2=dc, t3=tc, b2=bank: e.matmul(
                            psum[:, b2, :], wls[:, w2, d2, :],
                            qxf[d2][:, t3 * 512:(t3 + 1) * 512],
                            start=(d2 == 0), stop=(d2 == NDC - 1)),
                            waits=[d_w, qxtok[tc], evtok[bank]] if dc == 0 else [],
                            guard=False, inc=(dc == NDC - 1))
                    es = (tc % 2) + 2 * ui
                    ev = p.op("vector", lambda e, b2=bank, e2=es: e.tensor_scalar(
                        evb[:, e2, :], psum[:, b2, :], 1.0, None, ALU.mult, ALU.max,
                        accum_out=tmax[:, b2:b2 + 1]),
                        waits=[gtok, sptok[ui]])
                    evtok[bank] = ev
                    p.op("vector", lambda e, u2=ui, b2=bank: e.tensor_tensor(
                        runm[:, u2:u2 + 1], runm[:, u2:u2 + 1], tmax[:, b2:b2 + 1],
                        ALU.max))
                    if tc % 2 == 1:
                        sptok[ui] = p.dma_op(
                            "gpsimd", spd.ap()[ic, tc - 1:tc + 1].rearrange("t p n -> p t n"),
                            evb[:, 2 * ui:2 * ui + 2, :], waits=[ev])
            mictok[ic] = gtok
        g_s1 = p.wd("gpsimd")
        d_sc = allreduce_maxes(2, 0, 14)
        p.op("vector", lambda e: e.tensor_tensor(scal[:, 5:6], scal[:, 14:15], scal[:, 0:1],
                                                 ALU.mult), waits=[d_sc])
        p.op("vector", lambda e: e.tensor_scalar(scal[:, 5:6], scal[:, 5:6], 1.0 / QMAX,
                                                 1e-12, ALU.mult, ALU.max))
        p.op("vector", lambda e: e.reciprocal(scal[:, 6:7], scal[:, 5:6]))
        p.op("vector", lambda e: e.tensor_tensor(scal[:, 6:7], scal[:, 6:7], scal[:, 0:1],
                                                 ALU.mult))
        p.op("vector", lambda e: e.tensor_tensor(scal[:, 2:3], scal[:, 15:16], scal[:, 0:1],
                                                 ALU.mult))
        p.op("vector", lambda e: e.tensor_scalar(scal[:, 2:3], scal[:, 2:3], 1.0 / QMAX,
                                                 1e-12, ALU.mult, ALU.max))
        p.op("vector", lambda e: e.reciprocal(scal[:, 4:5], scal[:, 2:3]))
        p.op("vector", lambda e: e.tensor_tensor(scal[:, 4:5], scal[:, 4:5], scal[:, 0:1],
                                                 ALU.mult))
        p.op("vector", lambda e: e.tensor_scalar(scal[:, 3:4], scal[:, 2:3], -C_RNE, None,
                                                 ALU.mult))
        p.op("vector", lambda e: e.tensor_scalar(scal[:, 7:8], scal[:, 2:3], 1.0 / FQ, None,
                                                 ALU.mult))
        vz = p.op("vector", lambda e: e.tensor_scalar(runm[:, 2:3], runm[:, 2:3], 0.0, None,
                                                      ALU.mult), waits=[p.wv("gpsimd")])
        rdtok = [None, None]
        sptk = [None, None]
        s1tok = [None, None]
        sttk = [None, None]
        pend = None

        def s2_post(g):
            sl, ic, gh, sv2 = g
            p.op("vector", lambda e, s2=sl: e.tensor_scalar(
                gwk[:, s2, :], gwk[:, s2, :], C_RNE + 65535.0, -C_RNE,
                ALU.min, ALU.add),
                waits=[sv2])
            vv3 = p.op("vector", lambda e, s2=sl: e.scalar_tensor_tensor(
                gld[:, s2, :], kbf[:, s2, :], -C_RNE, gwk[:, s2, :],
                ALU.add, ALU.mult))
            sttk[sl] = vv3
            rdtok[sl] = p.op("vector", lambda e, s2=sl: e.tensor_reduce(
                mx1[:, :], gld[:, s2, :], AX.X, ALU.max, apply_absolute_value=True))
            p.op("vector", lambda e: e.tensor_tensor(runm[:, 2:3], runm[:, 2:3],
                                                     mx1[:, :], ALU.max))
            sptk[sl] = p.dma_op("gpsimd",
                                sp_P[ic, 4 * gh:4 * gh + 4].rearrange("t p n -> p t n"),
                                gld[:, sl, :].rearrange("p (t n) -> p t n", n=512),
                                waits=[vv3])

        for idx in range(2 * NIC):
            ic, gh = idx // 2, idx % 2
            sl = idx % 2
            d_g = p.dma_op("sync", gld[:, sl, :].rearrange("p (t n) -> p t n", n=512),
                           sp_gate[ic, 4 * gh:4 * gh + 4].rearrange("t p n -> p t n"),
                           waits=[g_s1 if idx < 2 else None,
                                  rdtok[sl], sptk[sl], vz,
                                  p.wv("tensor") if idx == 0 else None])
            vv1 = p.op("vector", lambda e, s2=sl: e.tensor_scalar(
                kbf[:, s2, :], gld[:, s2, :], scal[:, 4:5], C_RNE, ALU.mult, ALU.add),
                waits=[d_g, s1tok[sl]])
            sv1 = p.op("scalar", lambda e, s2=sl: e.activation(
                gwk[:, s2, :], kbf[:, s2, :], AF.Sigmoid, scale=scal[:, 2:3],
                bias=scal[:, 3:4]),
                waits=[vv1, sttk[sl]])
            s1tok[sl] = sv1
            sv2 = p.op("scalar", lambda e, s2=sl: e.activation(
                gwk[:, s2, :], gwk[:, s2, :], AF.Identity, scale=cst[:, 4:5],
                bias=cst[:, 2:3]))
            if pend is not None:
                s2_post(pend)
            pend = (sl, ic, gh, sv2)
        s2_post(pend)
        g_s2 = p.wd("gpsimd")
        d_sc = allreduce_maxes(1, 2, 14)
        p.op("vector", lambda e: e.tensor_tensor(scal[:, 8:9], scal[:, 14:15], scal[:, 7:8],
                                                 ALU.mult), waits=[d_sc])
        p.op("vector", lambda e: e.tensor_scalar(scal[:, 8:9], scal[:, 8:9], 1.0 / QMAX,
                                                 1e-12, ALU.mult, ALU.max))
        p.op("vector", lambda e: e.reciprocal(scal[:, 9:10], scal[:, 8:9]))
        p.op("vector", lambda e: e.tensor_tensor(scal[:, 9:10], scal[:, 9:10], scal[:, 7:8],
                                                 ALU.mult))
        p.op("vector", lambda e: e.tensor_tensor(scal[:, 10:11], scal[:, 8:9], scal[:, 5:6],
                                                 ALU.mult))
        vz = p.op("vector", lambda e: e.tensor_scalar(runm[:, 3:4], runm[:, 3:4], 0.0, None,
                                                      ALU.mult), waits=[p.wv("gpsimd")])
        rdtok = [None, None]
        sptk = [None, None]
        s1tok = [None, None]
        sttk = [None, None]
        pend = None

        def s3_post(g):
            sl, ic, gh, sv2 = g
            vv2 = p.op("vector", lambda e, s2=sl: e.scalar_tensor_tensor(
                gld[:, s2, :], kbf[:, s2, :], -C_RNE, gwk[:, s2, :],
                ALU.add, ALU.mult),
                waits=[sv2])
            sttk[sl] = vv2
            rdtok[sl] = p.op("vector", lambda e, s2=sl: e.tensor_reduce(
                mx1[:, :], gld[:, s2, :], AX.X, ALU.max, apply_absolute_value=True))
            p.op("vector", lambda e: e.tensor_tensor(runm[:, 3:4], runm[:, 3:4],
                                                     mx1[:, :], ALU.max))
            sptk[sl] = p.dma_op(
                "gpsimd",
                sp_P2.ap().rearrange("t i p n -> i t p n")[ic, 4 * gh:4 * gh + 4]
                .rearrange("t p n -> p t n"),
                gld[:, sl, :].rearrange("p (t n) -> p t n", n=512),
                waits=[vv2])

        for idx in range(2 * NIC):
            ic, gh = idx // 2, idx % 2
            sl = idx % 2
            d_p = p.dma_op("sync", gld[:, sl, :].rearrange("p (t n) -> p t n", n=512),
                           sp_P[ic, 4 * gh:4 * gh + 4].rearrange("t p n -> p t n"),
                           waits=[g_s2 if idx < 2 else None,
                                  rdtok[sl], sptk[sl], vz,
                                  p.wv("scalar") if idx == 0 else None])
            d_u = p.dma_op("sync", uld[:, sl, :].rearrange("p (t n) -> p t n", n=512),
                           sp_up[ic, 4 * gh:4 * gh + 4].rearrange("t p n -> p t n"),
                           waits=[s1tok[sl]])
            vv1 = p.op("vector", lambda e, s2=sl: e.tensor_scalar(
                kbf[:, s2, :], gld[:, s2, :], scal[:, 9:10], C_RNE, ALU.mult, ALU.add),
                waits=[d_p])
            sv1 = p.op("scalar", lambda e, s2=sl: e.activation(
                gwk[:, s2, :], uld[:, s2, :], AF.Identity, scale=scal[:, 6:7],
                bias=cst[:, 2:3]),
                waits=[d_u, sttk[sl]])
            sv2 = p.op("scalar", lambda e, s2=sl: e.activation(
                gwk[:, s2, :], gwk[:, s2, :], AF.Identity, bias=cst[:, 3:4]))
            s1tok[sl] = sv2
            if pend is not None:
                s3_post(pend)
            pend = (sl, ic, gh, sv2)
        s3_post(pend)
        g_s3 = p.wd("gpsimd")
        wt1 = p.wv("tensor")
        xrl = [p.dma_op("sync", xs[4 + k][:, :], sp_xq[k], waits=[wt1])
               for k in range(4)]
        d_sc = allreduce_maxes(1, 3, 14)
        p.op("vector", lambda e: e.tensor_tensor(scal[:, 11:12], scal[:, 14:15],
                                                 scal[:, 10:11], ALU.mult),
             waits=[d_sc])
        p.op("vector", lambda e: e.tensor_scalar(scal[:, 11:12], scal[:, 11:12], 1.0 / QMAX,
                                                 1e-12, ALU.mult, ALU.max))
        p.op("vector", lambda e: e.reciprocal(scal[:, 12:13], scal[:, 11:12]))
        p.op("vector", lambda e: e.tensor_tensor(scal[:, 12:13], scal[:, 12:13],
                                                 scal[:, 10:11], ALU.mult))
        vsc = p.op("vector", lambda e: e.tensor_scalar(scal[:, 13:14], scal[:, 11:12], 0.5,
                                                       None, ALU.mult))
        kotok = [None, None]
        wdhtok = [None, None, None, None]
        rstok = {2: None, 3: None, 4: None, 5: None}
        kobfree = None
        for tp in range(4):
            for ic in range(NIC):
                for h in range(2):
                    tc = 2 * tp + h
                    sl = (2 * ic + h) % 2
                    d_p2 = p.dma_op("sync", xlc[:, sl, :], sp_P2[tc, ic],
                                    waits=[g_s3 if (tp == 0 and ic == 0) else None,
                                           kotok[sl]])
                    scr = xlb[:, sl, :]
                    tvk = p.op("vector", lambda e, s2=sl, sc2=scr: e.tensor_scalar(
                        sc2, xlc[:, s2, :], scal[:, 12:13], C_RNE,
                        ALU.mult, ALU.add),
                        waits=[d_p2, vsc, kotok[sl]])
                    kotok[sl] = p.op("scalar", lambda e, i2=ic, h2=h, sc2=scr: e.activation(
                        kob[:, i2, h2 * 512:(h2 + 1) * 512], sc2, AF.Identity,
                        bias=cst[:, 3:4]),
                        waits=[tvk, kobfree])
            vko = p.wv("scalar")
            gA = gB = None
            for hc in range(NHC):
                bA = 2 + 2 * (hc % 2)
                bB = bA + 1
                slots = ((2 * hc) % 4, (2 * hc + 1) % 4)
                for ih in range(2):
                    slot = slots[ih]
                    d_w = p.dma_op("sync", wdh[:, slot, :, :],
                                   wqT_d[hc, 12 * ih:12 * ih + 12].rearrange("i p n -> p i n"),
                                   waits=[wdhtok[slot]])
                    for i12 in range(12):
                        ic = 12 * ih + i12
                        gA = p.op("tensor", lambda e, b2=bA, s2=slot, i2=i12, ic2=ic: e.matmul(
                            psum[:, b2, :], wdh[:, s2, i2, :], kob[:, ic2, 0:512],
                            start=(ic2 == 0), stop=(ic2 == NIC - 1)),
                            waits=[d_w, vko, rstok[bA]] if ic == 0 else
                                  ([d_w] if i12 == 0 else []),
                            guard=False, inc=(ic == NIC - 1))
                        gB = p.op("tensor", lambda e, b2=bB, s2=slot, i2=i12, ic2=ic: e.matmul(
                            psum[:, b2, :], wdh[:, s2, i2, :], kob[:, ic2, 512:1024],
                            start=(ic2 == 0), stop=(ic2 == NIC - 1)),
                            waits=[rstok[bB]] if ic == 0 else [],
                            guard=False, inc=(ic == NIC - 1))
                wdhtok[slots[0]] = wdhtok[slots[1]] = gB
                rstok[bA] = p.op("vector", lambda e, h2=hc, b2=bA, t3=2 * tp: e.scalar_tensor_tensor(
                    xs[h2][:, t3 * 512:(t3 + 1) * 512], psum[:, b2, :], scal[:, 13:14],
                    xs[h2][:, t3 * 512:(t3 + 1) * 512], ALU.mult, ALU.add),
                    waits=[gA, xrl[hc - 4] if hc >= 4 else None])
                rstok[bB] = p.op("vector", lambda e, h2=hc, b2=bB, t3=2 * tp + 1: e.scalar_tensor_tensor(
                    xs[h2][:, t3 * 512:(t3 + 1) * 512], psum[:, b2, :], scal[:, 13:14],
                    xs[h2][:, t3 * 512:(t3 + 1) * 512], ALU.mult, ALU.add),
                    waits=[gB])
            kobfree = gB

    def emit_ssm():
        dptok = iptok = None
        for qh in range(4):
            xbftok = None
            for tq2 in range(2):
                tc = 2 * qh + tq2
                sv = ln_stats_tc(tc, True, r_q[:, tq2 * 512:(tq2 + 1) * 512])
                p.op("vector", lambda e, t2=tq2: e.tensor_tensor(
                    rm_q[:, t2 * 512:(t2 + 1) * 512], r_q[:, t2 * 512:(t2 + 1) * 512],
                    mt[:, :], ALU.mult), waits=[sv])
            v_ln = p.wv("vector")
            g_xbf = p.wd("gpsimd")
            wt0 = p.wv("tensor")
            d_x = None
            for tq2 in range(2):
                d_x = p.dma_op("sync", xkb[:, :, tq2 * 512:(tq2 + 1) * 512],
                               sp_xbf[2 * qh + tq2].rearrange("d p n -> p d n"),
                               waits=[g_xbf, wt0])
            wsltok = None
            for ec in range(NEC):
                d_wd = p.dma_op("sync", wsl[:, 0, :, :],
                                wT_dec[ec].rearrange("c p n -> p c n"),
                                waits=[wsltok, init_g if ec == 0 else None])
                d_wi = p.dma_op("sync", wsl[:, 1, :, :],
                                wT_inj[ec].rearrange("c p n -> p c n"))
                for tq2 in range(2):
                    tv0 = tv = None
                    for dc in range(NDC):
                        tv0 = p.op("tensor", lambda e, d2=dc, t2=tq2: e.matmul(
                            psum[:, 0, :], wsl[:, 0, d2, :],
                            xkb[:, d2, t2 * 512:(t2 + 1) * 512],
                            start=(d2 == 0), stop=(d2 == NDC - 1)),
                            waits=[d_x, d_wi, dptok] if dc == 0 else [],
                            guard=False, inc=(dc == NDC - 1))
                        tv = p.op("tensor", lambda e, d2=dc, t2=tq2: e.matmul(
                            psum[:, 1, :], wsl[:, 1, d2, :],
                            xkb[:, d2, t2 * 512:(t2 + 1) * 512],
                            start=(d2 == 0), stop=(d2 == NDC - 1)),
                            waits=[iptok] if dc == 0 else [],
                            guard=False, inc=(dc == NDC - 1))
                    dptok = p.op("vector", lambda e, t2=tq2: e.tensor_scalar(
                        dpre[:, t2 * 512:(t2 + 1) * 512], psum[:, 0, :], 1.0, None,
                        ALU.mult), waits=[tv0, tv])
                    iptok = p.op("vector", lambda e, t2=tq2: e.tensor_scalar(
                        ipre[:, t2 * 512:(t2 + 1) * 512], psum[:, 1, :], 1.0, None,
                        ALU.mult))
                wsltok = tv
                p.op("vector", lambda e: e.tensor_tensor(dpre[:, :], dpre[:, :], r_q[:, :],
                                                         ALU.mult),
                     waits=[v_ln])
                p.op("vector", lambda e, e2=ec: e.scalar_tensor_tensor(
                    dpre[:, :], rm_q[:, :], csdn[:, e2:e2 + 1], dpre[:, :],
                    ALU.mult, ALU.add))
                sv = p.op("scalar", lambda e, e2=ec: e.activation(
                    dpre[:, :], dpre[:, :], AF.Sigmoid, bias=bdec[:, e2:e2 + 1]),
                    waits=[p.wv("vector")])
                p.op("vector", lambda e: e.tensor_tensor(ipre[:, :], ipre[:, :], r_q[:, :],
                                                         ALU.mult))
                vv = p.op("vector", lambda e, e2=ec: e.scalar_tensor_tensor(
                    ipre[:, :], rm_q[:, :], csin[:, e2:e2 + 1], ipre[:, :],
                    ALU.mult, ALU.add))
                sv2 = p.op("scalar", lambda e, e2=ec: e.activation(
                    ipre[:, :], ipre[:, :], AF.Identity, bias=binj[:, e2:e2 + 1]),
                    waits=[vv])
                p.op("vector", lambda e, e2=ec: e.tensor_tensor_scan(
                    ipre[:, :], dpre[:, :], ipre[:, :], carr[:, e2:e2 + 1],
                    ALU.mult, ALU.add),
                    waits=[sv, sv2])
                p.op("vector", lambda e, e2=ec: e.tensor_copy(carr[:, e2:e2 + 1],
                                                              ipre[:, 1023:1024]))
                p.op("vector", lambda e, e2=ec, q2=qh: e.tensor_tensor(
                    xs[e2][:, q2 * 1024:(q2 + 1) * 1024],
                    xs[e2][:, q2 * 1024:(q2 + 1) * 1024], ipre[:, :], ALU.add))

    from collections import deque
    units = deque(dq_units)
    for k in range(T // 128):
        in_unit(k)
        if k % 2 == 1 and units:
            units.popleft()()
    pre1 = None
    if n_stages >= 1:
        u_ln, f_ln = ln1_make()
        for tc in range(NTC):
            u_ln(tc)
            for _ in range(5):
                if units:
                    units.popleft()()
        pre1 = f_ln()
    while units:
        units.popleft()()
    p.op("vector", lambda e: e.tensor_scalar(carr[:, :], carr[:, :], 0.0, None, ALU.mult))
    init_g = p.wd("gpsimd")

    if n_stages >= 1:
        emit_ffn(pre1)
    if n_stages >= 2:
        emit_ssm()
    if n_stages >= 3:
        emit_ffn()

    vfin = p.wv("vector")
    psttok = None
    odtok = [None, None]
    for tc in range(T // 128):
        sv = None
        for g4 in range(2):
            tvs = None
            for d4 in range(4):
                dc = 4 * g4 + d4
                tvs = p.op("tensor", lambda e, d2=dc, t2=tc, ps2=d4: e.transpose(
                    psT[:, ps2 * 128:(ps2 + 1) * 128],
                    xs[d2][:, t2 * 128:(t2 + 1) * 128], idf[:, :]),
                    waits=[vfin if (tc == 0 and dc == 0) else None,
                           psttok if d4 == 0 else None],
                    guard=False, inc=(d4 == 3))
            sv = psttok = p.op("scalar", lambda e, g2=g4, sl=(tc % 2): e.copy(
                obuf[:, sl, g2 * 512:(g2 + 1) * 512], psT[:, 0:512]),
                waits=[tvs, odtok[tc % 2] if g4 == 0 else None])
        odtok[tc % 2] = p.dma_op("sync", y_out[tc * 128:(tc + 1) * 128, :],
                                 obuf[:, tc % 2, :], waits=[sv])

    final_loads = p.wd("sync")
    with ExitStack() as semstack:
        sems = {}
        for nm in ("opvector", "opscalar", "optensor", "opsync", "opgpsimd", "cc"):
            sems[nm] = semstack.enter_context(nc.semaphore(nm))
        for e in ("sync", "gpsimd"):
            for i in range(Prog.NDSEM):
                key = "d" + e + str(i)
                sems[key] = semstack.enter_context(nc.semaphore(key))
        with nc.Block() as block:
            @block.sync
            def _(sync):
                for f in p.q["sync"]:
                    f(sync, sems)
                for (key, val) in final_loads:
                    sync.wait_ge(sems[key], val)

            @block.vector
            def _(vector):
                for f in p.q["vector"]:
                    f(vector, sems)

            @block.scalar
            def _(scalar):
                for f in p.q["scalar"]:
                    f(scalar, sems)

            @block.tensor
            def _(tensor):
                for f in p.q["tensor"]:
                    f(tensor, sems)

            @block.gpsimd
            def _(gpsimd):
                for f in p.q["gpsimd"]:
                    f(gpsimd, sems)
    stack.close()
    return nc


def _prep_inputs(L0_attn_x_seq, L0_attn_W_decay, L0_attn_b_decay, L0_attn_W_input,
                 L0_attn_b_input, L0_ffn_w_gate, L0_ffn_w_up, L0_ffn_w_down):
    ident_f = np.eye(128, dtype=np.float32)
    ident_b = np.eye(128, dtype=ml_dtypes.bfloat16)
    ones_b = np.ones((128, 128), dtype=ml_dtypes.bfloat16)
    bdec = np.ascontiguousarray(np.asarray(L0_attn_b_decay, np.float32).reshape(NEC, 128).T)
    binj = np.ascontiguousarray(np.asarray(L0_attn_b_input, np.float32).reshape(NEC, 128).T)
    shared = {
        "w_up": np.ascontiguousarray(np.asarray(L0_ffn_w_up, np.float32)),
        "w_gate": np.ascontiguousarray(np.asarray(L0_ffn_w_gate, np.float32)),
        "w_down": np.ascontiguousarray(np.asarray(L0_ffn_w_down, np.float32)),
        "w_dec": np.ascontiguousarray(np.asarray(L0_attn_W_decay, np.float32)),
        "w_inj": np.ascontiguousarray(np.asarray(L0_attn_W_input, np.float32)),
        "b_dec": bdec, "b_inj": binj,
        "cones": ones_b, "cidf": ident_f, "cidb": ident_b,
        "ccst": np.tile(np.array([EPS, -0.5, C_RNE, -C_RNE, FQ, 0.0, 0.0, 0.0],
                                 np.float32), (128, 1)),
    }
    x = np.asarray(L0_attn_x_seq, np.float32)
    return [dict(shared, x_in=np.ascontiguousarray(x[c])) for c in range(8)]


def kernel(**inputs):
    in_maps = _prep_inputs(**inputs)
    nc = build_program(N_STAGES)
    res = run_bass_kernel_spmd(nc, in_maps, core_ids=list(range(8)))
    return np.stack([np.asarray(res.results[c]["y"]) for c in range(8)], axis=0)


# version 07
# speedup vs baseline: 1.1270x; 1.1270x over previous
import numpy as np
import ml_dtypes
import concourse.bass as bass
import concourse.mybir as mybir
from concourse.bass_utils import run_bass_kernel_spmd

F32 = mybir.dt.float32
BF16 = mybir.dt.bfloat16
AF = mybir.ActivationFunctionType
ALU = mybir.AluOpType
AX = mybir.AxisListType

T, D, I = 4096, 1024, 3072
NDC, NIC, NHC, NEC = 8, 24, 8, 8
NTC = 8
C_RNE = 12582912.0
QMAX = 32767.0
FQ = 65536.0
EPS = 1e-5
N_STAGES = 3


class Prog:

    ENGS = ("sync", "vector", "scalar", "tensor", "gpsimd")
    NDSEM = 6

    def __init__(self):
        self.q = {e: [] for e in self.ENGS}
        self.n = {e: 0 for e in self.ENGS}
        self.di = {e: 0 for e in self.ENGS}
        self.dc = {e: [0] * self.NDSEM for e in self.ENGS}
        self.cc = 0

    @staticmethod
    def _flat(waits):
        out = []
        for w in waits:
            if w is None:
                continue
            if isinstance(w, list):
                out.extend(x for x in w if x is not None)
            else:
                out.append(w)
        return out

    def op(self, eng, f, waits=(), guard=True, inc=True):
        prev = self.n[eng] if guard else 0
        if inc:
            self.n[eng] += 1
        cnt = self.n[eng]
        ws = self._flat(waits)

        def emit(engine, sems):
            for (key, val) in ws:
                if val > 0:
                    engine.wait_ge(sems[key], val)
            if prev > 0:
                engine.wait_ge(sems["op" + eng], prev)
            ins = f(engine)
            if inc:
                ins.then_inc(sems["op" + eng], 1)
        self.q[eng].append(emit)
        return ("op" + eng, cnt)

    def dma_op(self, eng, out, in_, waits=()):
        i = self.di[eng] % self.NDSEM
        self.di[eng] += 1
        self.dc[eng][i] += 16
        cnt = self.dc[eng][i]
        key = "d" + eng + str(i)
        ws = self._flat(waits)

        def emit(engine, sems):
            for (k2, val) in ws:
                if val > 0:
                    engine.wait_ge(sems[k2], val)
            if cnt > 16:
                engine.wait_ge(sems[key], cnt - 16)
            engine.dma_start(out=out, in_=in_).then_inc(sems[key], 16)
        self.q[eng].append(emit)
        return (key, cnt)

    def cc_op(self, ins_ap, outs_ap, waits=()):
        self.cc += 1
        cnt = self.cc
        ws = self._flat(waits)

        def emit(engine, sems):
            for (key, val) in ws:
                if val > 0:
                    engine.wait_ge(sems[key], val)
            engine.collective_compute(
                "AllReduce", ALU.max, replica_groups=[list(range(8))],
                ins=[ins_ap], outs=[outs_ap]).then_inc(sems["cc"], 1)
        self.q["gpsimd"].append(emit)
        return ("cc", cnt)

    def wv(self, eng):
        return ("op" + eng, self.n[eng])

    def wd(self, eng):
        return [("d" + eng + str(i), self.dc[eng][i])
                for i in range(self.NDSEM) if self.dc[eng][i] > 0]


def build_program(n_stages=N_STAGES, debug=False):
    nc = bass.Bass("TRN2", target_bir_lowering=False, debug=False)
    p = Prog()

    x_in = nc.dram_tensor("x_in", [T, D], F32, kind="ExternalInput").ap()
    y_out = nc.dram_tensor("y", [T, D], F32, kind="ExternalOutput").ap()
    wu_in = nc.dram_tensor("w_up", [I, D], F32, kind="ExternalInput").ap()
    wg_in = nc.dram_tensor("w_gate", [I, D], F32, kind="ExternalInput").ap()
    wd_in = nc.dram_tensor("w_down", [D, I], F32, kind="ExternalInput").ap()
    wdec_in = nc.dram_tensor("w_dec", [D, D], F32, kind="ExternalInput").ap()
    winj_in = nc.dram_tensor("w_inj", [D, D], F32, kind="ExternalInput").ap()
    bdec_in = nc.dram_tensor("b_dec", [128, NEC], F32, kind="ExternalInput").ap()
    binj_in = nc.dram_tensor("b_inj", [128, NEC], F32, kind="ExternalInput").ap()
    ones_in = nc.dram_tensor("cones", [128, 128], BF16, kind="ExternalInput").ap()
    idf_in = nc.dram_tensor("cidf", [128, 128], F32, kind="ExternalInput").ap()
    idb_in = nc.dram_tensor("cidb", [128, 128], BF16, kind="ExternalInput").ap()
    cst_in = nc.dram_tensor("ccst", [128, 8], F32, kind="ExternalInput").ap()

    dram = lambda name, shape, dt=F32: nc.dram_tensor(name, shape, dt)
    ddram = (lambda name, shape, dt=F32: nc.dram_tensor(name, shape, dt,
             kind="ExternalOutput")) if debug else dram
    sp_xln = ddram("sp_xln", [NTC, NDC, 128, 512], BF16)
    sp_up = ddram("sp_up", [NIC, NTC, 128, 512], BF16)
    sp_gate = ddram("sp_gate", [NIC, NTC, 128, 512], BF16)
    sp_P = ddram("sp_P", [NIC, NTC, 128, 512], BF16)
    sp_P2 = ddram("sp_P2", [NTC, NIC, 128, 512], BF16)
    sp_xbf = dram("sp_xbf", [NTC, NDC, 128, 512], BF16)
    sp_xq = dram("sp_xq", [4, 128, T])
    wqT_u = dram("wqT_u", [NIC, NDC, 128, 128], BF16)
    wqT_g = dram("wqT_g", [NIC, NDC, 128, 128], BF16)
    wqT_d = dram("wqT_d", [NHC, NIC, 128, 128], BF16)
    wT_dec = dram("wT_dec", [NEC, NDC, 128, 128], BF16)
    wT_inj = dram("wT_inj", [NEC, NDC, 128, 128], BF16)
    dbg_out = (nc.dram_tensor("dbg", [128, 16], F32, kind="ExternalOutput").ap()
               if debug else None)
    ncc = 16
    cc_i = [dram(f"cc_i{k}", [1, 4]) for k in range(ncc)]
    cc_o = [dram(f"cc_o{k}", [1, 4]) for k in range(ncc)]

    from contextlib import ExitStack
    stack = ExitStack()
    sb = lambda name, shape, dt=F32: stack.enter_context(nc.sbuf_tensor(name, shape, dt))
    xs = [sb(f"x{dc}", [128, T]) for dc in range(NDC)]
    ones = sb("ones", [128, 128], BF16)
    idf = sb("idf", [128, 128], F32)
    idb = sb("idb", [128, 128], BF16)
    bdec = sb("bdec", [128, NEC])
    binj = sb("binj", [128, NEC])
    csdn = sb("csdn", [128, NEC])
    csin = sb("csin", [128, NEC])
    tev = sb("tev", [128, 8, 128], BF16)
    tmax = sb("tmax", [128, 8])
    runm = sb("runm", [128, 4])
    scal = sb("scal", [128, 16])
    mt = sb("mt", [128, 512])
    rt = sb("rt", [128, 512])
    vt = sb("vt", [128, 512])
    xlb = sb("xlb", [128, 2, 512])
    xlc = sb("xlc", [128, 2, 512], BF16)
    xlbB = lambda s: xlc[:, s, :]
    xbfb = sb("xbfb", [128, 2, 512], BF16)
    xsqb = sb("xsqb", [128, 2, 512], BF16)
    carr = sb("carr", [128, NEC])
    mx1 = sb("mx1", [128, 1])
    mxc = sb("mxc", [4, 1])
    cst = sb("cst", [128, 8])
    AR = 15360
    arena = sb("arena", [128, AR])

    def carve(off, n):
        assert off + n <= AR, (off, n)
        return arena[:, off:off + n], off + n

    o = 0
    stage, o = carve(o, 2048)
    stg = stage.rearrange("p (s n) -> p s n", n=1024)
    wdq, o = carve(o, 1536)
    wdqb_r, o = carve(o, 768)
    wdqb = wdqb_r.bitcast(BF16)
    srep_r, o = carve(o, 1536)
    rrep_r, o = carve(o, 1536)
    srep = srep_r.rearrange("p (b k) -> p b k", k=32)
    rrep = rrep_r.rearrange("p (b k) -> p b k", k=32)
    mxs, o = carve(o, 48)
    sss, o = carve(o, 48)
    rrr, o = carve(o, 48)

    o = 0
    xg_r, o = carve(o, 2048)
    xgv = xg_r.bitcast(BF16).rearrange("p (d n) -> p d n", n=512)
    evb_r, o = carve(o, 1024)
    evb = evb_r.bitcast(BF16).rearrange("p (s n) -> p s n", n=512)
    wls_r, o = carve(o, 2048)
    wls = wls_r.bitcast(BF16).rearrange("p (s d n) -> p s d n", s=4, d=NDC)

    o = 0
    gld_r, o = carve(o, 3072)
    gld = gld_r.bitcast(BF16).rearrange("p (s n) -> p s n", n=2048)
    gwk_r, o = carve(o, 4096)
    gwk = gwk_r.rearrange("p (s n) -> p s n", n=2048)
    kbf_r, o = carve(o, 4096)
    kbf = kbf_r.rearrange("p (s n) -> p s n", n=2048)
    uld_r, o = carve(o, 2048)
    uld = uld_r.bitcast(BF16).rearrange("p (s n) -> p s n", n=2048)
    spb_r, o = carve(o, 2048)
    spb = spb_r.bitcast(BF16).rearrange("p (s n) -> p s n", n=2048)

    o = 0
    kob_r, o = carve(o, 12288)
    kob = kob_r.bitcast(BF16).rearrange("p (i n) -> p i n", n=1024)
    wdh_r, o = carve(o, 3072)
    wdh = wdh_r.bitcast(BF16).rearrange("p (s i n) -> p s i n", s=4, i=12)

    o = 0
    r_q, o = carve(o, 1024)
    rm_q, o = carve(o, 1024)
    dpre, o = carve(o, 1024)
    ipre, o = carve(o, 1024)
    xkb_r, o = carve(o, 4096)
    xkb = xkb_r.bitcast(BF16).rearrange("p (d n) -> p d n", n=1024)
    wsl_r, o = carve(o, 1024)
    wsl = wsl_r.bitcast(BF16).rearrange("p (w d n) -> p w d n", w=2, d=NDC)

    obuf = arena[:, 0:2048].rearrange("p (s n) -> p s n", n=1024)

    psum = stack.enter_context(nc.psum_tensor("psum", [128, 6, 512], F32))
    psT = stack.enter_context(nc.psum_tensor("psT", [128, 512], F32))
    psB = stack.enter_context(nc.psum_tensor("psB", [128, 1024], BF16))

    p.dma_op("sync", ones[:, :], ones_in)
    p.dma_op("sync", idf[:, :], idf_in)
    p.dma_op("sync", idb[:, :], idb_in)
    p.dma_op("sync", bdec[:, :], bdec_in)
    d_c2 = p.dma_op("sync", binj[:, :], binj_in)
    d_c2 = p.dma_op("sync", cst[:, :], cst_in)

    in_state = {"ev": [None, None], "ps": None}

    def in_unit(tc):
        slot = tc % 2
        d_st = p.dma_op("sync", stg[:, slot, :], x_in[tc * 128:(tc + 1) * 128, :],
                        waits=[in_state["ev"][slot], d_c2 if tc == 0 else None])
        for dc in range(NDC):
            ps = (tc * NDC + dc) % 4
            tv = p.op("tensor", lambda e, sl=slot, d2=dc, ps2=ps: e.transpose(
                psT[:, ps2 * 128:(ps2 + 1) * 128], stg[:, sl, d2 * 128:(d2 + 1) * 128],
                idf[:, :]),
                waits=[d_st, in_state["ps"]])
            in_state["ps"] = p.op("scalar", lambda e, d2=dc, t2=tc, ps2=ps: e.copy(
                xs[d2][:, t2 * 128:(t2 + 1) * 128], psT[:, ps2 * 128:(ps2 + 1) * 128]),
                waits=[tv])
        in_state["ev"][slot] = in_state["ps"]

    def dequant_tile(src_ap, ncols):
        nblk = ncols // 32
        d_w = p.dma_op("sync", wdq[:, 0:ncols], src_ap,
                       waits=[p.wv("vector"), p.wv("tensor")])
        v3 = wdq[:, 0:ncols].rearrange("p (b k) -> p b k", k=32)
        p.op("vector", lambda e: e.tensor_reduce(mxs[:, 0:nblk], v3, AX.X, ALU.max,
                                                 apply_absolute_value=True),
             waits=[d_w])
        p.op("vector", lambda e: e.tensor_scalar(sss[:, 0:nblk], mxs[:, 0:nblk],
                                                 1.0 / 7.0, 1e-12, ALU.mult, ALU.max))
        p.op("vector", lambda e: e.reciprocal(rrr[:, 0:nblk], sss[:, 0:nblk]))
        p.op("vector", lambda e: e.tensor_copy(srep[:, 0:nblk, 0:1], sss[:, 0:nblk, None]))
        p.op("vector", lambda e: e.tensor_copy(rrep[:, 0:nblk, 0:1], rrr[:, 0:nblk, None]))
        w = 1
        while w < 32:
            p.op("vector", lambda e, w2=w: e.tensor_copy(srep[:, 0:nblk, w2:2 * w2],
                                                         srep[:, 0:nblk, 0:w2]))
            p.op("vector", lambda e, w2=w: e.tensor_copy(rrep[:, 0:nblk, w2:2 * w2],
                                                         rrep[:, 0:nblk, 0:w2]))
            w *= 2
        p.op("vector", lambda e: e.tensor_tensor(v3, v3, rrep[:, 0:nblk, :], ALU.mult))
        p.op("vector", lambda e: e.tensor_scalar(wdq[:, 0:ncols], wdq[:, 0:ncols],
                                                 C_RNE, -C_RNE, ALU.add, ALU.add))
        p.op("vector", lambda e: e.tensor_tensor(v3, v3, srep[:, 0:nblk, :], ALU.mult))
        return p.op("vector", lambda e: e.tensor_copy(wdqb[:, 0:ncols], wdq[:, 0:ncols]),
                    waits=[tr_state["lastT"]])

    tr_state = {"psB": None, "tev": None, "lastT": None}

    def transpose_out(ncols, dst_gfn, v_ready):
        nch = ncols // 128
        j = 0
        while j < nch:
            g = min(8, nch - j)
            tvs = None
            for q in range(g):
                tvs = p.op("tensor", lambda e, j2=j + q, q2=q: e.transpose(
                    psB[:, q2 * 128:(q2 + 1) * 128], wdqb[:, j2 * 128:(j2 + 1) * 128],
                    idb[:, :]),
                    waits=[v_ready, tr_state["psB"]] if q == 0 else [],
                    guard=False, inc=(q == g - 1))
            sv = p.op("scalar", lambda e, g2=g: e.copy(
                tev[:, 0:g2, :], psB[:, 0:g2 * 128]),
                waits=[tvs, tr_state["tev"]])
            tr_state["psB"] = sv
            tr_state["tev"] = p.dma_op("gpsimd", dst_gfn(j, g), tev[:, 0:g, :],
                                       waits=[sv])
            j += g
        tr_state["lastT"] = tvs

    dq_units = []

    def _dq_ug(w_src, dst, ic):
        v_r = dequant_tile(w_src[ic * 128:(ic + 1) * 128, :], D)
        transpose_out(D, lambda j, g, ic2=ic, d2=dst:
                      d2.ap()[ic2, j:j + g].rearrange("c p n -> p c n"), v_r)

    def _dq_dn(hc, ih):
        v_r = dequant_tile(wd_in[hc * 128:(hc + 1) * 128, ih * 1536:(ih + 1) * 1536], 1536)
        transpose_out(1536, lambda j, g, h2=hc, i0=12 * ih:
                      wqT_d.ap()[h2, i0 + j:i0 + j + g].rearrange("c p n -> p c n"), v_r)

    def _dq_ssm(w_src, dstT, cs_t, ec):
        d_w = p.dma_op("sync", wdq[:, 0:D], w_src[ec * 128:(ec + 1) * 128, :],
                       waits=[p.wv("vector"), p.wv("tensor")])
        vb = p.op("vector", lambda e: e.tensor_copy(wdqb[:, 0:D], wdq[:, 0:D]),
                  waits=[d_w, tr_state["lastT"]])
        p.op("vector", lambda e, ec2=ec, c2=cs_t: e.tensor_reduce(
            c2[:, ec2:ec2 + 1], wdqb[:, 0:D], AX.X, ALU.add))
        p.op("vector", lambda e, ec2=ec, c2=cs_t: e.tensor_scalar(
            c2[:, ec2:ec2 + 1], c2[:, ec2:ec2 + 1], -1.0, None, ALU.mult))
        transpose_out(D, lambda j, g, e2=ec, d2=dstT:
                      d2.ap()[e2, j:j + g].rearrange("c p n -> p c n"), vb)

    for (w_src, dst) in ((wu_in, wqT_u), (wg_in, wqT_g)):
        for ic in range(NIC):
            dq_units.append(lambda w2=w_src, d2=dst, i2=ic: _dq_ug(w2, d2, i2))
    for hc in range(NHC):
        for ih in range(2):
            dq_units.append(lambda h2=hc, i2=ih: _dq_dn(h2, i2))
    for (w_src, dstT, cs_t) in ((wdec_in, wT_dec, csdn), (winj_in, wT_inj, csin)):
        for ec in range(NEC):
            dq_units.append(lambda w2=w_src, d2=dstT, c2=cs_t, e2=ec: _dq_ssm(w2, d2, c2, e2))

    bno = [0]

    def allreduce_maxes(m, src_col, dst_col):
        k = bno[0]
        bno[0] += 1
        tv = p.op("tensor", lambda e: e.transpose(psT[0:m, 0:128],
                                                  runm[:, src_col:src_col + m], idf[:, :]),
                  waits=[p.wv("vector"), p.wv("scalar"), p.wv("gpsimd")])
        vv = p.op("vector", lambda e: e.tensor_reduce(mxc[0:m, 0:1], psT[0:m, 0:128],
                                                      AX.X, ALU.max),
                  waits=[tv])
        d_up = p.dma_op("sync", cc_i[k].ap()[0:1, 0:m].rearrange("o m -> m o"),
                        mxc[0:m, 0:1], waits=[vv])
        cv = p.cc_op(cc_i[k].ap()[0:1, 0:m], cc_o[k].ap()[0:1, 0:m],
                     waits=[d_up])
        return p.dma_op("sync", scal[:, dst_col:dst_col + m],
                        cc_o[k].ap()[0:1, 0:m].broadcast_to([128, m]),
                        waits=[cv])

    def ln_stats_tc(tc, spill_xbf, r_dst):
        tv = None
        mmx = [0] * NDC
        mmx2 = [0] * NDC
        for dc in range(NDC):
            sl = dc % 2
            vv = p.op("vector", lambda e, d2=dc, s2=sl: e.tensor_copy(
                xbfb[:, s2, :], xs[d2][:, tc * 512:(tc + 1) * 512]),
                waits=[mmx[dc - 2] if dc >= 2 else p.wv("tensor"),
                       p.wv("scalar"),
                       p.wd("gpsimd") if spill_xbf else None])
            sv = p.op("scalar", lambda e, d2=dc, s2=sl: e.activation(
                xsqb[:, s2, :], xs[d2][:, tc * 512:(tc + 1) * 512], AF.Square),
                waits=[mmx2[dc - 2] if dc >= 2 else p.wv("tensor")])
            mmx[dc] = p.op("tensor", lambda e, s2=sl, d2=dc: e.matmul(
                psum[:, 0, :], ones[:, :], xbfb[:, s2, :],
                start=(d2 == 0), stop=(d2 == NDC - 1)),
                waits=[vv], guard=False)
            tv = mmx2[dc] = p.op("tensor", lambda e, s2=sl, d2=dc: e.matmul(
                psum[:, 1, :], ones[:, :], xsqb[:, s2, :],
                start=(d2 == 0), stop=(d2 == NDC - 1)),
                waits=[sv], guard=False)
            if spill_xbf:
                p.dma_op("gpsimd", sp_xbf[tc, dc], xbfb[:, sl, :],
                         waits=[vv])
        p.op("vector", lambda e: e.tensor_scalar(mt[:, :], psum[:, 0, :], 1.0 / D, None,
                                                 ALU.mult), waits=[tv])
        p.op("vector", lambda e: e.tensor_scalar(vt[:, :], psum[:, 1, :], 1.0 / D, None,
                                                 ALU.mult))
        p.op("vector", lambda e: e.scalar_tensor_tensor(rt[:, :], mt[:, :], -1.0, mt[:, :],
                                                        ALU.mult, ALU.mult))
        vv = p.op("vector", lambda e: e.tensor_tensor(vt[:, :], vt[:, :], rt[:, :], ALU.add))
        p.op("scalar", lambda e: e.activation(vt[:, :], vt[:, :], AF.Ln, bias=cst[:, 0:1]),
             waits=[vv])
        return p.op("scalar", lambda e, rd=r_dst: e.activation(rd, vt[:, :], AF.Exp,
                                                               scale=cst[:, 1:2]))

    def ln1_make():
        st = {"tok": [None, None]}
        p.op("vector", lambda e: e.tensor_scalar(runm[:, 0:4], runm[:, 0:4], 0.0, None,
                                                 ALU.mult))

        def unit(tc):
            sv = ln_stats_tc(tc, False, rt[:, :])
            for dc in range(NDC):
                sl = dc % 2
                p.op("vector", lambda e, d2=dc, s2=sl, t3=tc: e.tensor_tensor(
                    xlb[:, s2, :], xs[d2][:, t3 * 512:(t3 + 1) * 512], mt[:, :],
                    ALU.subtract),
                    waits=[sv, st["tok"][sl]])
                vv = p.op("vector", lambda e, s2=sl: e.tensor_tensor(
                    xlbB(s2), xlb[:, s2, :], rt[:, :], ALU.mult))
                p.op("vector", lambda e, s2=sl, d2=dc: e.tensor_reduce(
                    tmax[:, d2:d2 + 1], xlbB(s2), AX.X, ALU.max,
                    apply_absolute_value=True))
                st["tok"][sl] = p.dma_op("gpsimd", sp_xln[tc, dc], xlbB(sl),
                                         waits=[vv])
            p.op("vector", lambda e: e.tensor_reduce(mx1[:, :], tmax[:, 0:NDC], AX.X,
                                                     ALU.max))
            p.op("vector", lambda e: e.tensor_tensor(runm[:, 0:1], runm[:, 0:1],
                                                     mx1[:, :], ALU.max))

        def finish():
            vxw = p.wv("vector")
            sxw = p.wv("scalar")
            xsp = [p.dma_op("gpsimd", sp_xq[k], xs[4 + k][:, :], waits=[vxw, sxw])
                   for k in range(4)]
            return {"xsp": xsp, "g_ln": p.wd("gpsimd")}
        return unit, finish

    def emit_ffn(pre=None):
        qxf = [xs[4 + dc // 2][:, :].bitcast(BF16)[:, (dc % 2) * T:(dc % 2 + 1) * T]
               for dc in range(NDC)]
        if pre is None:
            u_ln, f_ln = ln1_make()
            for tc in range(NTC):
                u_ln(tc)
            pre = f_ln()
        xsp = pre["xsp"]
        g_ln = pre["g_ln"]
        d_sc = allreduce_maxes(1, 0, 14)
        p.op("vector", lambda e: e.tensor_scalar(scal[:, 0:1], scal[:, 14:15], 1.0 / QMAX,
                                                 1e-12, ALU.mult, ALU.max),
             waits=[d_sc])
        p.op("vector", lambda e: e.reciprocal(scal[:, 1:2], scal[:, 0:1]))
        vz = p.op("vector", lambda e: e.tensor_scalar(runm[:, 0:2], runm[:, 0:2], 0.0,
                                                      None, ALU.mult))
        ts1tok = None
        a2tok = [None, None]
        qxtok = [None] * NTC
        for tc in range(NTC):
            d_x = p.dma_op("sync", xgv[:, :, :], sp_xln[tc].rearrange("d p n -> p d n"),
                           waits=[g_ln, ts1tok] +
                                 ([p.wv("vector"), p.wv("tensor")] if tc == 0 else []))
            for dc in range(NDC):
                sl = dc % 2
                tv1 = p.op("vector", lambda e, d2=dc, s2=sl: e.tensor_scalar(
                    xlb[:, s2, :], xgv[:, d2, :], scal[:, 1:2], C_RNE, ALU.mult, ALU.add),
                    waits=[d_x, vz, a2tok[sl]])
                a2tok[sl] = p.op("scalar", lambda e, d2=dc, s2=sl, t3=tc: e.activation(
                    qxf[d2][:, t3 * 512:(t3 + 1) * 512], xlb[:, s2, :], AF.Identity,
                    bias=cst[:, 3:4]),
                    waits=[tv1, xsp[dc // 2]])
            ts1tok = tv1
            qxtok[tc] = a2tok[1]
        evtok = {0: None, 1: None, 2: None, 3: None}
        sptok = [None, None]
        mictok = [None] * NIC
        for ic in range(NIC):
            ws0, ws1 = (2 * ic) % 4, (2 * ic + 1) % 4
            d_wu = p.dma_op("sync", wls[:, ws0, :, :],
                            wqT_u[ic].rearrange("c p n -> p c n"),
                            waits=[mictok[ic - 2] if ic >= 2 else None,
                                   init_g if ic == 0 else None,
                                   p.wv("vector") if ic == 0 else None])
            d_wg = p.dma_op("sync", wls[:, ws1, :, :],
                            wqT_g[ic].rearrange("c p n -> p c n"))
            gtok = None
            for tc in range(NTC):
                for (ui, d_w, ws2, spd) in ((0, d_wu, ws0, sp_up), (1, d_wg, ws1, sp_gate)):
                    bank = 2 * (tc % 2) + ui
                    for dc in range(NDC):
                        gtok = p.op("tensor", lambda e, w2=ws2, d2=dc, t3=tc, b2=bank: e.matmul(
                            psum[:, b2, :], wls[:, w2, d2, :],
                            qxf[d2][:, t3 * 512:(t3 + 1) * 512],
                            start=(d2 == 0), stop=(d2 == NDC - 1)),
                            waits=[d_w, qxtok[tc], evtok[bank]] if dc == 0 else [],
                            guard=False, inc=(dc == NDC - 1))
                    es = (tc % 2) + 2 * ui
                    ev = p.op("vector", lambda e, b2=bank, e2=es: e.tensor_scalar(
                        evb[:, e2, :], psum[:, b2, :], 1.0, None, ALU.mult, ALU.max,
                        accum_out=tmax[:, b2:b2 + 1]),
                        waits=[gtok, sptok[ui]])
                    evtok[bank] = ev
                    p.op("vector", lambda e, u2=ui, b2=bank: e.tensor_tensor(
                        runm[:, u2:u2 + 1], runm[:, u2:u2 + 1], tmax[:, b2:b2 + 1],
                        ALU.max))
                    if tc % 2 == 1:
                        sptok[ui] = p.dma_op(
                            "gpsimd", spd.ap()[ic, tc - 1:tc + 1].rearrange("t p n -> p t n"),
                            evb[:, 2 * ui:2 * ui + 2, :], waits=[ev])
            mictok[ic] = gtok
        g_s1 = p.wd("gpsimd")
        d_sc = allreduce_maxes(2, 0, 14)
        p.op("vector", lambda e: e.tensor_tensor(scal[:, 5:6], scal[:, 14:15], scal[:, 0:1],
                                                 ALU.mult), waits=[d_sc])
        p.op("vector", lambda e: e.tensor_scalar(scal[:, 5:6], scal[:, 5:6], 1.0 / QMAX,
                                                 1e-12, ALU.mult, ALU.max))
        p.op("vector", lambda e: e.reciprocal(scal[:, 6:7], scal[:, 5:6]))
        p.op("vector", lambda e: e.tensor_tensor(scal[:, 6:7], scal[:, 6:7], scal[:, 0:1],
                                                 ALU.mult))
        p.op("vector", lambda e: e.tensor_tensor(scal[:, 2:3], scal[:, 15:16], scal[:, 0:1],
                                                 ALU.mult))
        p.op("vector", lambda e: e.tensor_scalar(scal[:, 2:3], scal[:, 2:3], 1.0 / QMAX,
                                                 1e-12, ALU.mult, ALU.max))
        p.op("vector", lambda e: e.reciprocal(scal[:, 4:5], scal[:, 2:3]))
        p.op("vector", lambda e: e.tensor_tensor(scal[:, 4:5], scal[:, 4:5], scal[:, 0:1],
                                                 ALU.mult))
        p.op("vector", lambda e: e.tensor_scalar(scal[:, 3:4], scal[:, 2:3], -C_RNE, None,
                                                 ALU.mult))
        p.op("vector", lambda e: e.tensor_scalar(scal[:, 7:8], scal[:, 2:3], 1.0 / FQ, None,
                                                 ALU.mult))
        vz = p.op("vector", lambda e: e.tensor_scalar(runm[:, 2:3], runm[:, 2:3], 0.0, None,
                                                      ALU.mult), waits=[p.wv("gpsimd")])
        NG = 2 * NIC
        kgtok = [None, None, None]
        sigtok = [None, None]
        stttok = [None, None]
        spltok = [None, None]
        dgq = [None] * NG

        def s2_load(n):
            ic, gh = n // 2, n % 2
            dgq[n] = p.dma_op("sync",
                              gld[:, n % 3, :].rearrange("p (t n) -> p t n", n=512),
                              sp_gate[ic, 4 * gh:4 * gh + 4].rearrange("t p n -> p t n"),
                              waits=[g_s1 if n < 3 else None, kgtok[n % 3], vz,
                                     p.wv("tensor") if n == 0 else None])

        def s2_post(n, sv2):
            w, s = n % 2, n % 2
            p.op("vector", lambda e, w2=w: e.tensor_scalar(
                gwk[:, w2, :], gwk[:, w2, :], C_RNE + 65535.0, -C_RNE,
                ALU.min, ALU.add),
                waits=[sv2])
            vv3 = p.op("vector", lambda e, w2=w, s2=s: e.scalar_tensor_tensor(
                spb[:, s2, :], kbf[:, w2, :], -C_RNE, gwk[:, w2, :],
                ALU.add, ALU.mult),
                waits=[spltok[s], g_s1 if n < 2 else None])
            stttok[w] = vv3
            p.op("vector", lambda e, s2=s: e.tensor_reduce(
                mx1[:, :], spb[:, s2, :], AX.X, ALU.max, apply_absolute_value=True))
            p.op("vector", lambda e: e.tensor_tensor(runm[:, 2:3], runm[:, 2:3],
                                                     mx1[:, :], ALU.max))
            ic, gh = n // 2, n % 2
            spltok[s] = p.dma_op("gpsimd",
                                 sp_P[ic, 4 * gh:4 * gh + 4].rearrange("t p n -> p t n"),
                                 spb[:, s, :].rearrange("p (t n) -> p t n", n=512),
                                 waits=[vv3])

        s2_load(0)
        s2_load(1)
        for n in range(NG):
            if n + 2 < NG:
                s2_load(n + 2)
            w = n % 2
            kgtok[n % 3] = p.op("vector", lambda e, w2=w, l2=n % 3: e.tensor_scalar(
                kbf[:, w2, :], gld[:, l2, :], scal[:, 4:5], C_RNE, ALU.mult, ALU.add),
                waits=[dgq[n], sigtok[w]])
            sigtok[w] = p.op("scalar", lambda e, w2=w: e.activation(
                gwk[:, w2, :], kbf[:, w2, :], AF.Sigmoid, scale=scal[:, 2:3],
                bias=scal[:, 3:4]),
                waits=[kgtok[n % 3], stttok[w]])
            sv2 = p.op("scalar", lambda e, w2=w: e.activation(
                gwk[:, w2, :], gwk[:, w2, :], AF.Identity, scale=cst[:, 4:5],
                bias=cst[:, 2:3]))
            if n > 0:
                s2_post(n - 1, prev_sv2)
            prev_sv2 = sv2
        s2_post(NG - 1, prev_sv2)
        g_s2 = p.wd("gpsimd")
        d_sc = allreduce_maxes(1, 2, 14)
        p.op("vector", lambda e: e.tensor_tensor(scal[:, 8:9], scal[:, 14:15], scal[:, 7:8],
                                                 ALU.mult), waits=[d_sc])
        p.op("vector", lambda e: e.tensor_scalar(scal[:, 8:9], scal[:, 8:9], 1.0 / QMAX,
                                                 1e-12, ALU.mult, ALU.max))
        p.op("vector", lambda e: e.reciprocal(scal[:, 9:10], scal[:, 8:9]))
        p.op("vector", lambda e: e.tensor_tensor(scal[:, 9:10], scal[:, 9:10], scal[:, 7:8],
                                                 ALU.mult))
        p.op("vector", lambda e: e.tensor_tensor(scal[:, 10:11], scal[:, 8:9], scal[:, 5:6],
                                                 ALU.mult))
        vz = p.op("vector", lambda e: e.tensor_scalar(runm[:, 3:4], runm[:, 3:4], 0.0, None,
                                                      ALU.mult), waits=[p.wv("gpsimd")])
        NG = 2 * NIC
        kgtok = [None, None, None]
        kutok = [None, None]
        sigtok = [None, None]
        stttok = [None, None]
        spltok = [None, None]
        dgq = [None] * NG
        duq = [None] * NG

        def s3_load(n):
            ic, gh = n // 2, n % 2
            dgq[n] = p.dma_op("sync",
                              gld[:, n % 3, :].rearrange("p (t n) -> p t n", n=512),
                              sp_P[ic, 4 * gh:4 * gh + 4].rearrange("t p n -> p t n"),
                              waits=[g_s2 if n < 3 else None, kgtok[n % 3], vz,
                                     p.wv("scalar") if n == 0 else None])

        def s3_uload(n):
            ic, gh = n // 2, n % 2
            duq[n] = p.dma_op("sync",
                              uld[:, n % 2, :].rearrange("p (t n) -> p t n", n=512),
                              sp_up[ic, 4 * gh:4 * gh + 4].rearrange("t p n -> p t n"),
                              waits=[g_s2 if n < 2 else None, kutok[n % 2]])

        def s3_post(n, sv2):
            w, s = n % 2, n % 2
            vv2 = p.op("vector", lambda e, w2=w, s2=s: e.scalar_tensor_tensor(
                spb[:, s2, :], kbf[:, w2, :], -C_RNE, gwk[:, w2, :],
                ALU.add, ALU.mult),
                waits=[sv2, spltok[s], g_s2 if n < 2 else None])
            stttok[w] = vv2
            p.op("vector", lambda e, s2=s: e.tensor_reduce(
                mx1[:, :], spb[:, s2, :], AX.X, ALU.max, apply_absolute_value=True))
            p.op("vector", lambda e: e.tensor_tensor(runm[:, 3:4], runm[:, 3:4],
                                                     mx1[:, :], ALU.max))
            ic, gh = n // 2, n % 2
            spltok[s] = p.dma_op(
                "gpsimd",
                sp_P2.ap().rearrange("t i p n -> i t p n")[ic, 4 * gh:4 * gh + 4]
                .rearrange("t p n -> p t n"),
                spb[:, s, :].rearrange("p (t n) -> p t n", n=512),
                waits=[vv2])

        s3_load(0)
        s3_load(1)
        s3_uload(0)
        for n in range(NG):
            if n + 2 < NG:
                s3_load(n + 2)
            w = n % 2
            kgtok[n % 3] = p.op("vector", lambda e, w2=w, l2=n % 3: e.tensor_scalar(
                kbf[:, w2, :], gld[:, l2, :], scal[:, 9:10], C_RNE, ALU.mult, ALU.add),
                waits=[dgq[n]])
            sv1 = p.op("scalar", lambda e, w2=w: e.activation(
                gwk[:, w2, :], uld[:, w2, :], AF.Identity, scale=scal[:, 6:7],
                bias=cst[:, 2:3]),
                waits=[duq[n], stttok[w]])
            sv2 = p.op("scalar", lambda e, w2=w: e.activation(
                gwk[:, w2, :], gwk[:, w2, :], AF.Identity, bias=cst[:, 3:4]))
            kutok[w] = sv2
            if n + 1 < NG:
                s3_uload(n + 1)
            if n > 0:
                s3_post(n - 1, prev_sv2)
            prev_sv2 = sv2
        s3_post(NG - 1, prev_sv2)
        g_s3 = p.wd("gpsimd")
        wt1 = p.wv("tensor")
        xrl = [p.dma_op("sync", xs[4 + k][:, :], sp_xq[k], waits=[wt1])
               for k in range(4)]
        d_sc = allreduce_maxes(1, 3, 14)
        p.op("vector", lambda e: e.tensor_tensor(scal[:, 11:12], scal[:, 14:15],
                                                 scal[:, 10:11], ALU.mult),
             waits=[d_sc])
        p.op("vector", lambda e: e.tensor_scalar(scal[:, 11:12], scal[:, 11:12], 1.0 / QMAX,
                                                 1e-12, ALU.mult, ALU.max))
        p.op("vector", lambda e: e.reciprocal(scal[:, 12:13], scal[:, 11:12]))
        p.op("vector", lambda e: e.tensor_tensor(scal[:, 12:13], scal[:, 12:13],
                                                 scal[:, 10:11], ALU.mult))
        vsc = p.op("vector", lambda e: e.tensor_scalar(scal[:, 13:14], scal[:, 11:12], 0.5,
                                                       None, ALU.mult))
        kotok = [None, None]
        wdhtok = [None, None, None, None]
        rstok = {2: None, 3: None, 4: None, 5: None}
        kobfree = None
        for tp in range(4):
            for ic in range(NIC):
                for h in range(2):
                    tc = 2 * tp + h
                    sl = (2 * ic + h) % 2
                    d_p2 = p.dma_op("sync", xlc[:, sl, :], sp_P2[tc, ic],
                                    waits=[g_s3 if (tp == 0 and ic == 0) else None,
                                           kotok[sl]])
                    scr = xlb[:, sl, :]
                    tvk = p.op("vector", lambda e, s2=sl, sc2=scr: e.tensor_scalar(
                        sc2, xlc[:, s2, :], scal[:, 12:13], C_RNE,
                        ALU.mult, ALU.add),
                        waits=[d_p2, vsc, kotok[sl]])
                    kotok[sl] = p.op("scalar", lambda e, i2=ic, h2=h, sc2=scr: e.activation(
                        kob[:, i2, h2 * 512:(h2 + 1) * 512], sc2, AF.Identity,
                        bias=cst[:, 3:4]),
                        waits=[tvk, kobfree])
            vko = p.wv("scalar")
            gA = gB = None
            for hc in range(NHC):
                bA = 2 + 2 * (hc % 2)
                bB = bA + 1
                slots = ((2 * hc) % 4, (2 * hc + 1) % 4)
                for ih in range(2):
                    slot = slots[ih]
                    d_w = p.dma_op("sync", wdh[:, slot, :, :],
                                   wqT_d[hc, 12 * ih:12 * ih + 12].rearrange("i p n -> p i n"),
                                   waits=[wdhtok[slot],
                                          g_s3 if (tp == 0 and hc < 2) else None])
                    for i12 in range(12):
                        ic = 12 * ih + i12
                        gA = p.op("tensor", lambda e, b2=bA, s2=slot, i2=i12, ic2=ic: e.matmul(
                            psum[:, b2, :], wdh[:, s2, i2, :], kob[:, ic2, 0:512],
                            start=(ic2 == 0), stop=(ic2 == NIC - 1)),
                            waits=[d_w, vko, rstok[bA]] if ic == 0 else
                                  ([d_w] if i12 == 0 else []),
                            guard=False, inc=(ic == NIC - 1))
                        gB = p.op("tensor", lambda e, b2=bB, s2=slot, i2=i12, ic2=ic: e.matmul(
                            psum[:, b2, :], wdh[:, s2, i2, :], kob[:, ic2, 512:1024],
                            start=(ic2 == 0), stop=(ic2 == NIC - 1)),
                            waits=[rstok[bB]] if ic == 0 else [],
                            guard=False, inc=(ic == NIC - 1))
                wdhtok[slots[0]] = wdhtok[slots[1]] = gB
                rstok[bA] = p.op("vector", lambda e, h2=hc, b2=bA, t3=2 * tp: e.scalar_tensor_tensor(
                    xs[h2][:, t3 * 512:(t3 + 1) * 512], psum[:, b2, :], scal[:, 13:14],
                    xs[h2][:, t3 * 512:(t3 + 1) * 512], ALU.mult, ALU.add),
                    waits=[gA, xrl[hc - 4] if hc >= 4 else None])
                rstok[bB] = p.op("vector", lambda e, h2=hc, b2=bB, t3=2 * tp + 1: e.scalar_tensor_tensor(
                    xs[h2][:, t3 * 512:(t3 + 1) * 512], psum[:, b2, :], scal[:, 13:14],
                    xs[h2][:, t3 * 512:(t3 + 1) * 512], ALU.mult, ALU.add),
                    waits=[gB])
            kobfree = gB

    def emit_ssm():
        dptok = iptok = None
        for qh in range(4):
            xbftok = None
            for tq2 in range(2):
                tc = 2 * qh + tq2
                sv = ln_stats_tc(tc, True, r_q[:, tq2 * 512:(tq2 + 1) * 512])
                p.op("vector", lambda e, t2=tq2: e.tensor_tensor(
                    rm_q[:, t2 * 512:(t2 + 1) * 512], r_q[:, t2 * 512:(t2 + 1) * 512],
                    mt[:, :], ALU.mult), waits=[sv])
            v_ln = p.wv("vector")
            g_xbf = p.wd("gpsimd")
            wt0 = p.wv("tensor")
            d_x = None
            for tq2 in range(2):
                d_x = p.dma_op("sync", xkb[:, :, tq2 * 512:(tq2 + 1) * 512],
                               sp_xbf[2 * qh + tq2].rearrange("d p n -> p d n"),
                               waits=[g_xbf, wt0])
            wsltok = None
            for ec in range(NEC):
                d_wd = p.dma_op("sync", wsl[:, 0, :, :],
                                wT_dec[ec].rearrange("c p n -> p c n"),
                                waits=[wsltok, init_g if ec == 0 else None])
                d_wi = p.dma_op("sync", wsl[:, 1, :, :],
                                wT_inj[ec].rearrange("c p n -> p c n"))
                for tq2 in range(2):
                    tv0 = tv = None
                    for dc in range(NDC):
                        tv0 = p.op("tensor", lambda e, d2=dc, t2=tq2: e.matmul(
                            psum[:, 0, :], wsl[:, 0, d2, :],
                            xkb[:, d2, t2 * 512:(t2 + 1) * 512],
                            start=(d2 == 0), stop=(d2 == NDC - 1)),
                            waits=[d_x, d_wi, dptok] if dc == 0 else [],
                            guard=False, inc=(dc == NDC - 1))
                        tv = p.op("tensor", lambda e, d2=dc, t2=tq2: e.matmul(
                            psum[:, 1, :], wsl[:, 1, d2, :],
                            xkb[:, d2, t2 * 512:(t2 + 1) * 512],
                            start=(d2 == 0), stop=(d2 == NDC - 1)),
                            waits=[iptok] if dc == 0 else [],
                            guard=False, inc=(dc == NDC - 1))
                    dptok = p.op("vector", lambda e, t2=tq2: e.tensor_scalar(
                        dpre[:, t2 * 512:(t2 + 1) * 512], psum[:, 0, :], 1.0, None,
                        ALU.mult), waits=[tv0, tv])
                    iptok = p.op("vector", lambda e, t2=tq2: e.tensor_scalar(
                        ipre[:, t2 * 512:(t2 + 1) * 512], psum[:, 1, :], 1.0, None,
                        ALU.mult))
                wsltok = tv
                p.op("vector", lambda e: e.tensor_tensor(dpre[:, :], dpre[:, :], r_q[:, :],
                                                         ALU.mult),
                     waits=[v_ln])
                p.op("vector", lambda e, e2=ec: e.scalar_tensor_tensor(
                    dpre[:, :], rm_q[:, :], csdn[:, e2:e2 + 1], dpre[:, :],
                    ALU.mult, ALU.add))
                sv = p.op("scalar", lambda e, e2=ec: e.activation(
                    dpre[:, :], dpre[:, :], AF.Sigmoid, bias=bdec[:, e2:e2 + 1]),
                    waits=[p.wv("vector")])
                p.op("vector", lambda e: e.tensor_tensor(ipre[:, :], ipre[:, :], r_q[:, :],
                                                         ALU.mult))
                vv = p.op("vector", lambda e, e2=ec: e.scalar_tensor_tensor(
                    ipre[:, :], rm_q[:, :], csin[:, e2:e2 + 1], ipre[:, :],
                    ALU.mult, ALU.add))
                sv2 = p.op("scalar", lambda e, e2=ec: e.activation(
                    ipre[:, :], ipre[:, :], AF.Identity, bias=binj[:, e2:e2 + 1]),
                    waits=[vv])
                p.op("vector", lambda e, e2=ec: e.tensor_tensor_scan(
                    ipre[:, :], dpre[:, :], ipre[:, :], carr[:, e2:e2 + 1],
                    ALU.mult, ALU.add),
                    waits=[sv, sv2])
                p.op("vector", lambda e, e2=ec: e.tensor_copy(carr[:, e2:e2 + 1],
                                                              ipre[:, 1023:1024]))
                p.op("vector", lambda e, e2=ec, q2=qh: e.tensor_tensor(
                    xs[e2][:, q2 * 1024:(q2 + 1) * 1024],
                    xs[e2][:, q2 * 1024:(q2 + 1) * 1024], ipre[:, :], ALU.add))

    from collections import deque
    units = deque(dq_units)
    for k in range(T // 128):
        in_unit(k)
        if k % 2 == 1 and units:
            units.popleft()()
    pre1 = None
    if n_stages >= 1:
        u_ln, f_ln = ln1_make()
        for tc in range(NTC):
            u_ln(tc)
            for _ in range(5):
                if units:
                    units.popleft()()
        pre1 = f_ln()
    while units:
        units.popleft()()
    p.op("vector", lambda e: e.tensor_scalar(carr[:, :], carr[:, :], 0.0, None, ALU.mult))
    init_g = p.wd("gpsimd")

    if n_stages >= 1:
        emit_ffn(pre1)
    if n_stages >= 2:
        emit_ssm()
    if n_stages >= 3:
        emit_ffn()

    vfin = p.wv("vector")
    psttok = None
    odtok = [None, None]
    for tc in range(T // 128):
        sv = None
        for g4 in range(2):
            tvs = None
            for d4 in range(4):
                dc = 4 * g4 + d4
                tvs = p.op("tensor", lambda e, d2=dc, t2=tc, ps2=d4: e.transpose(
                    psT[:, ps2 * 128:(ps2 + 1) * 128],
                    xs[d2][:, t2 * 128:(t2 + 1) * 128], idf[:, :]),
                    waits=[vfin if (tc == 0 and dc == 0) else None,
                           psttok if d4 == 0 else None],
                    guard=False, inc=(d4 == 3))
            sv = psttok = p.op("scalar", lambda e, g2=g4, sl=(tc % 2): e.copy(
                obuf[:, sl, g2 * 512:(g2 + 1) * 512], psT[:, 0:512]),
                waits=[tvs, odtok[tc % 2] if g4 == 0 else None])
        odtok[tc % 2] = p.dma_op("sync", y_out[tc * 128:(tc + 1) * 128, :],
                                 obuf[:, tc % 2, :], waits=[sv])

    final_loads = p.wd("sync")
    with ExitStack() as semstack:
        sems = {}
        for nm in ("opvector", "opscalar", "optensor", "opsync", "opgpsimd", "cc"):
            sems[nm] = semstack.enter_context(nc.semaphore(nm))
        for e in ("sync", "gpsimd"):
            for i in range(Prog.NDSEM):
                key = "d" + e + str(i)
                sems[key] = semstack.enter_context(nc.semaphore(key))
        with nc.Block() as block:
            @block.sync
            def _(sync):
                for f in p.q["sync"]:
                    f(sync, sems)
                for (key, val) in final_loads:
                    sync.wait_ge(sems[key], val)

            @block.vector
            def _(vector):
                for f in p.q["vector"]:
                    f(vector, sems)

            @block.scalar
            def _(scalar):
                for f in p.q["scalar"]:
                    f(scalar, sems)

            @block.tensor
            def _(tensor):
                for f in p.q["tensor"]:
                    f(tensor, sems)

            @block.gpsimd
            def _(gpsimd):
                for f in p.q["gpsimd"]:
                    f(gpsimd, sems)
    stack.close()
    return nc


def _prep_inputs(L0_attn_x_seq, L0_attn_W_decay, L0_attn_b_decay, L0_attn_W_input,
                 L0_attn_b_input, L0_ffn_w_gate, L0_ffn_w_up, L0_ffn_w_down):
    ident_f = np.eye(128, dtype=np.float32)
    ident_b = np.eye(128, dtype=ml_dtypes.bfloat16)
    ones_b = np.ones((128, 128), dtype=ml_dtypes.bfloat16)
    bdec = np.ascontiguousarray(np.asarray(L0_attn_b_decay, np.float32).reshape(NEC, 128).T)
    binj = np.ascontiguousarray(np.asarray(L0_attn_b_input, np.float32).reshape(NEC, 128).T)
    shared = {
        "w_up": np.ascontiguousarray(np.asarray(L0_ffn_w_up, np.float32)),
        "w_gate": np.ascontiguousarray(np.asarray(L0_ffn_w_gate, np.float32)),
        "w_down": np.ascontiguousarray(np.asarray(L0_ffn_w_down, np.float32)),
        "w_dec": np.ascontiguousarray(np.asarray(L0_attn_W_decay, np.float32)),
        "w_inj": np.ascontiguousarray(np.asarray(L0_attn_W_input, np.float32)),
        "b_dec": bdec, "b_inj": binj,
        "cones": ones_b, "cidf": ident_f, "cidb": ident_b,
        "ccst": np.tile(np.array([EPS, -0.5, C_RNE, -C_RNE, FQ, 0.0, 0.0, 0.0],
                                 np.float32), (128, 1)),
    }
    x = np.asarray(L0_attn_x_seq, np.float32)
    return [dict(shared, x_in=np.ascontiguousarray(x[c])) for c in range(8)]


def kernel(**inputs):
    in_maps = _prep_inputs(**inputs)
    nc = build_program(N_STAGES)
    res = run_bass_kernel_spmd(nc, in_maps, core_ids=list(range(8)))
    return np.stack([np.asarray(res.results[c]["y"]) for c in range(8)], axis=0)


# version 08 (final)
# speedup vs baseline: 1.2288x; 1.0903x over previous
import numpy as np
import ml_dtypes
import concourse.bass as bass
import concourse.mybir as mybir
from concourse.bass_utils import run_bass_kernel_spmd

F32 = mybir.dt.float32
BF16 = mybir.dt.bfloat16
AF = mybir.ActivationFunctionType
ALU = mybir.AluOpType
AX = mybir.AxisListType

T, D, I = 4096, 1024, 3072
NDC, NIC, NHC, NEC = 8, 24, 8, 8
NTC = 8
C_RNE = 12582912.0
QMAX = 32767.0
FQ = 65536.0
EPS = 1e-5
N_STAGES = 3


class Prog:

    ENGS = ("sync", "vector", "scalar", "tensor", "gpsimd")
    NDSEM = 6

    def __init__(self):
        self.q = {e: [] for e in self.ENGS}
        self.n = {e: 0 for e in self.ENGS}
        self.di = {e: 0 for e in self.ENGS}
        self.dc = {e: [0] * self.NDSEM for e in self.ENGS}
        self.cc = 0

    @staticmethod
    def _flat(waits):
        out = []
        for w in waits:
            if w is None:
                continue
            if isinstance(w, list):
                out.extend(x for x in w if x is not None)
            else:
                out.append(w)
        return out

    def op(self, eng, f, waits=(), guard=True, inc=True):
        prev = self.n[eng] if guard else 0
        if inc:
            self.n[eng] += 1
        cnt = self.n[eng]
        ws = self._flat(waits)

        def emit(engine, sems):
            for (key, val) in ws:
                if val > 0:
                    engine.wait_ge(sems[key], val)
            if prev > 0:
                engine.wait_ge(sems["op" + eng], prev)
            ins = f(engine)
            if inc:
                ins.then_inc(sems["op" + eng], 1)
        self.q[eng].append(emit)
        return ("op" + eng, cnt)

    def dma_op(self, eng, out, in_, waits=()):
        i = self.di[eng] % self.NDSEM
        self.di[eng] += 1
        self.dc[eng][i] += 16
        cnt = self.dc[eng][i]
        key = "d" + eng + str(i)
        ws = self._flat(waits)

        def emit(engine, sems):
            for (k2, val) in ws:
                if val > 0:
                    engine.wait_ge(sems[k2], val)
            if cnt > 16:
                engine.wait_ge(sems[key], cnt - 16)
            engine.dma_start(out=out, in_=in_).then_inc(sems[key], 16)
        self.q[eng].append(emit)
        return (key, cnt)

    def cc_op(self, ins_ap, outs_ap, waits=()):
        self.cc += 1
        cnt = self.cc
        ws = self._flat(waits)

        def emit(engine, sems):
            for (key, val) in ws:
                if val > 0:
                    engine.wait_ge(sems[key], val)
            engine.collective_compute(
                "AllReduce", ALU.max, replica_groups=[list(range(8))],
                ins=[ins_ap], outs=[outs_ap]).then_inc(sems["cc"], 1)
        self.q["gpsimd"].append(emit)
        return ("cc", cnt)

    def wv(self, eng):
        return ("op" + eng, self.n[eng])

    def wd(self, eng):
        return [("d" + eng + str(i), self.dc[eng][i])
                for i in range(self.NDSEM) if self.dc[eng][i] > 0]


def build_program(n_stages=N_STAGES, debug=False):
    nc = bass.Bass("TRN2", target_bir_lowering=False, debug=False)
    p = Prog()

    x_in = nc.dram_tensor("x_in", [T, D], F32, kind="ExternalInput").ap()
    y_out = nc.dram_tensor("y", [T, D], F32, kind="ExternalOutput").ap()
    wu_in = nc.dram_tensor("w_up", [I, D], F32, kind="ExternalInput").ap()
    wg_in = nc.dram_tensor("w_gate", [I, D], F32, kind="ExternalInput").ap()
    wd_in = nc.dram_tensor("w_down", [D, I], F32, kind="ExternalInput").ap()
    wdec_in = nc.dram_tensor("w_dec", [D, D], F32, kind="ExternalInput").ap()
    winj_in = nc.dram_tensor("w_inj", [D, D], F32, kind="ExternalInput").ap()
    bdec_in = nc.dram_tensor("b_dec", [128, NEC], F32, kind="ExternalInput").ap()
    binj_in = nc.dram_tensor("b_inj", [128, NEC], F32, kind="ExternalInput").ap()
    ones_in = nc.dram_tensor("cones", [128, 128], BF16, kind="ExternalInput").ap()
    idf_in = nc.dram_tensor("cidf", [128, 128], F32, kind="ExternalInput").ap()
    idb_in = nc.dram_tensor("cidb", [128, 128], BF16, kind="ExternalInput").ap()
    cst_in = nc.dram_tensor("ccst", [128, 8], F32, kind="ExternalInput").ap()

    dram = lambda name, shape, dt=F32: nc.dram_tensor(name, shape, dt)
    ddram = (lambda name, shape, dt=F32: nc.dram_tensor(name, shape, dt,
             kind="ExternalOutput")) if debug else dram
    sp_xln = ddram("sp_xln", [NTC, NDC, 128, 512], BF16)
    sp_up = ddram("sp_up", [NIC, NTC, 128, 512], BF16)
    sp_gate = ddram("sp_gate", [NIC, NTC, 128, 512], BF16)
    sp_P = ddram("sp_P", [NIC, NTC, 128, 512], BF16)
    sp_P2 = ddram("sp_P2", [NTC, NIC, 128, 512], BF16)
    sp_xbf = dram("sp_xbf", [NTC, NDC, 128, 512], BF16)
    sp_xq = dram("sp_xq", [4, 128, T])
    wqT_u = dram("wqT_u", [NIC, NDC, 128, 128], BF16)
    wqT_g = dram("wqT_g", [NIC, NDC, 128, 128], BF16)
    wqT_d = dram("wqT_d", [NHC, NIC, 128, 128], BF16)
    wT_dec = dram("wT_dec", [NEC, NDC, 128, 128], BF16)
    wT_inj = dram("wT_inj", [NEC, NDC, 128, 128], BF16)
    dbg_out = (nc.dram_tensor("dbg", [128, 16], F32, kind="ExternalOutput").ap()
               if debug else None)
    ncc = 16
    cc_i = [dram(f"cc_i{k}", [1, 4]) for k in range(ncc)]
    cc_o = [dram(f"cc_o{k}", [1, 4]) for k in range(ncc)]

    from contextlib import ExitStack
    stack = ExitStack()
    sb = lambda name, shape, dt=F32: stack.enter_context(nc.sbuf_tensor(name, shape, dt))
    xs = [sb(f"x{dc}", [128, T]) for dc in range(NDC)]
    ones = sb("ones", [128, 128], BF16)
    idf = sb("idf", [128, 128], F32)
    idb = sb("idb", [128, 128], BF16)
    bdec = sb("bdec", [128, NEC])
    binj = sb("binj", [128, NEC])
    csdn = sb("csdn", [128, NEC])
    csin = sb("csin", [128, NEC])
    tev = sb("tev", [128, 8, 128], BF16)
    tmax = sb("tmax", [128, 8])
    runm = sb("runm", [128, 4])
    scal = sb("scal", [128, 16])
    mt = sb("mt", [128, 512])
    rt = sb("rt", [128, 512])
    vt = sb("vt", [128, 512])
    xlb = sb("xlb", [128, 2, 512])
    xlc = sb("xlc", [128, 2, 512], BF16)
    xlbB = lambda s: xlc[:, s, :]
    xbfb = sb("xbfb", [128, 2, 512], BF16)
    xsqb = sb("xsqb", [128, 2, 512], BF16)
    carr = sb("carr", [128, NEC])
    mx1 = sb("mx1", [128, 1])
    mxc = sb("mxc", [4, 1])
    cst = sb("cst", [128, 8])
    AR = 15360
    arena = sb("arena", [128, AR])

    def carve(off, n):
        assert off + n <= AR, (off, n)
        return arena[:, off:off + n], off + n

    o = 0
    stage, o = carve(o, 2048)
    stg = stage.rearrange("p (s n) -> p s n", n=1024)
    wdq, o = carve(o, 1536)
    wdqb_r, o = carve(o, 768)
    wdqb = wdqb_r.bitcast(BF16)
    srep_r, o = carve(o, 1536)
    rrep_r, o = carve(o, 1536)
    srep = srep_r.rearrange("p (b k) -> p b k", k=32)
    rrep = rrep_r.rearrange("p (b k) -> p b k", k=32)
    mxs, o = carve(o, 48)
    sss, o = carve(o, 48)
    rrr, o = carve(o, 48)

    o = 0
    xg_r, o = carve(o, 2048)
    xgv = xg_r.bitcast(BF16).rearrange("p (d n) -> p d n", n=512)
    evb_r, o = carve(o, 1024)
    evb = evb_r.bitcast(BF16).rearrange("p (s n) -> p s n", n=512)
    wls_r, o = carve(o, 2048)
    wls = wls_r.bitcast(BF16).rearrange("p (s d n) -> p s d n", s=4, d=NDC)

    o = 0
    gld_r, o = carve(o, 3072)
    gld = gld_r.bitcast(BF16).rearrange("p (s n) -> p s n", n=2048)
    gwk_r, o = carve(o, 4096)
    gwk = gwk_r.rearrange("p (s n) -> p s n", n=2048)
    kbf_r, o = carve(o, 4096)
    kbf = kbf_r.rearrange("p (s n) -> p s n", n=2048)
    uld_r, o = carve(o, 2048)
    uld = uld_r.bitcast(BF16).rearrange("p (s n) -> p s n", n=2048)
    spb_r, o = carve(o, 2048)
    spb = spb_r.bitcast(BF16).rearrange("p (s n) -> p s n", n=2048)

    o = 0
    kob_r, o = carve(o, 12288)
    kob2 = kob_r.bitcast(BF16).rearrange("p (b i n) -> p b i n", b=2, n=512)
    wdh_r, o = carve(o, 3072)
    wdh = wdh_r.bitcast(BF16).rearrange("p (s i n) -> p s i n", s=4, i=12)

    o = 0
    r_q, o = carve(o, 1024)
    rm_q, o = carve(o, 1024)
    dpre, o = carve(o, 1024)
    ipre, o = carve(o, 1024)
    xkb_r, o = carve(o, 4096)
    xkb = xkb_r.bitcast(BF16).rearrange("p (d n) -> p d n", n=1024)
    wsl_r, o = carve(o, 1024)
    wsl = wsl_r.bitcast(BF16).rearrange("p (w d n) -> p w d n", w=2, d=NDC)

    obuf = arena[:, 0:2048].rearrange("p (s n) -> p s n", n=1024)

    psum = stack.enter_context(nc.psum_tensor("psum", [128, 6, 512], F32))
    psT = stack.enter_context(nc.psum_tensor("psT", [128, 512], F32))
    psB = stack.enter_context(nc.psum_tensor("psB", [128, 1024], BF16))

    p.dma_op("sync", ones[:, :], ones_in)
    p.dma_op("sync", idf[:, :], idf_in)
    p.dma_op("sync", idb[:, :], idb_in)
    p.dma_op("sync", bdec[:, :], bdec_in)
    d_c2 = p.dma_op("sync", binj[:, :], binj_in)
    d_c2 = p.dma_op("sync", cst[:, :], cst_in)

    in_state = {"ev": [None, None], "ps": None}

    def in_unit(tc):
        slot = tc % 2
        d_st = p.dma_op("sync", stg[:, slot, :], x_in[tc * 128:(tc + 1) * 128, :],
                        waits=[in_state["ev"][slot], d_c2 if tc == 0 else None])
        for dc in range(NDC):
            ps = (tc * NDC + dc) % 4
            tv = p.op("tensor", lambda e, sl=slot, d2=dc, ps2=ps: e.transpose(
                psT[:, ps2 * 128:(ps2 + 1) * 128], stg[:, sl, d2 * 128:(d2 + 1) * 128],
                idf[:, :]),
                waits=[d_st, in_state["ps"]])
            in_state["ps"] = p.op("scalar", lambda e, d2=dc, t2=tc, ps2=ps: e.copy(
                xs[d2][:, t2 * 128:(t2 + 1) * 128], psT[:, ps2 * 128:(ps2 + 1) * 128]),
                waits=[tv])
        in_state["ev"][slot] = in_state["ps"]

    def dequant_tile(src_ap, ncols):
        nblk = ncols // 32
        d_w = p.dma_op("sync", wdq[:, 0:ncols], src_ap,
                       waits=[p.wv("vector"), p.wv("tensor")])
        v3 = wdq[:, 0:ncols].rearrange("p (b k) -> p b k", k=32)
        p.op("vector", lambda e: e.tensor_reduce(mxs[:, 0:nblk], v3, AX.X, ALU.max,
                                                 apply_absolute_value=True),
             waits=[d_w])
        p.op("vector", lambda e: e.tensor_scalar(sss[:, 0:nblk], mxs[:, 0:nblk],
                                                 1.0 / 7.0, 1e-12, ALU.mult, ALU.max))
        p.op("vector", lambda e: e.reciprocal(rrr[:, 0:nblk], sss[:, 0:nblk]))
        p.op("vector", lambda e: e.tensor_copy(srep[:, 0:nblk, 0:1], sss[:, 0:nblk, None]))
        p.op("vector", lambda e: e.tensor_copy(rrep[:, 0:nblk, 0:1], rrr[:, 0:nblk, None]))
        w = 1
        while w < 32:
            p.op("vector", lambda e, w2=w: e.tensor_copy(srep[:, 0:nblk, w2:2 * w2],
                                                         srep[:, 0:nblk, 0:w2]))
            p.op("vector", lambda e, w2=w: e.tensor_copy(rrep[:, 0:nblk, w2:2 * w2],
                                                         rrep[:, 0:nblk, 0:w2]))
            w *= 2
        p.op("vector", lambda e: e.tensor_tensor(v3, v3, rrep[:, 0:nblk, :], ALU.mult))
        p.op("vector", lambda e: e.tensor_scalar(wdq[:, 0:ncols], wdq[:, 0:ncols],
                                                 C_RNE, -C_RNE, ALU.add, ALU.add))
        p.op("vector", lambda e: e.tensor_tensor(v3, v3, srep[:, 0:nblk, :], ALU.mult))
        return p.op("vector", lambda e: e.tensor_copy(wdqb[:, 0:ncols], wdq[:, 0:ncols]),
                    waits=[tr_state["lastT"]])

    tr_state = {"psB": None, "tev": None, "lastT": None}

    def transpose_out(ncols, dst_gfn, v_ready):
        nch = ncols // 128
        j = 0
        while j < nch:
            g = min(8, nch - j)
            tvs = None
            for q in range(g):
                tvs = p.op("tensor", lambda e, j2=j + q, q2=q: e.transpose(
                    psB[:, q2 * 128:(q2 + 1) * 128], wdqb[:, j2 * 128:(j2 + 1) * 128],
                    idb[:, :]),
                    waits=[v_ready, tr_state["psB"]] if q == 0 else [],
                    guard=False, inc=(q == g - 1))
            sv = p.op("scalar", lambda e, g2=g: e.copy(
                tev[:, 0:g2, :], psB[:, 0:g2 * 128]),
                waits=[tvs, tr_state["tev"]])
            tr_state["psB"] = sv
            tr_state["tev"] = p.dma_op("gpsimd", dst_gfn(j, g), tev[:, 0:g, :],
                                       waits=[sv])
            j += g
        tr_state["lastT"] = tvs

    dq_units = []

    def _dq_ug(w_src, dst, ic):
        v_r = dequant_tile(w_src[ic * 128:(ic + 1) * 128, :], D)
        transpose_out(D, lambda j, g, ic2=ic, d2=dst:
                      d2.ap()[ic2, j:j + g].rearrange("c p n -> p c n"), v_r)

    def _dq_dn(hc, ih):
        v_r = dequant_tile(wd_in[hc * 128:(hc + 1) * 128, ih * 1536:(ih + 1) * 1536], 1536)
        transpose_out(1536, lambda j, g, h2=hc, i0=12 * ih:
                      wqT_d.ap()[h2, i0 + j:i0 + j + g].rearrange("c p n -> p c n"), v_r)

    def _dq_ssm(w_src, dstT, cs_t, ec):
        d_w = p.dma_op("sync", wdq[:, 0:D], w_src[ec * 128:(ec + 1) * 128, :],
                       waits=[p.wv("vector"), p.wv("tensor")])
        vb = p.op("vector", lambda e: e.tensor_copy(wdqb[:, 0:D], wdq[:, 0:D]),
                  waits=[d_w, tr_state["lastT"]])
        p.op("vector", lambda e, ec2=ec, c2=cs_t: e.tensor_reduce(
            c2[:, ec2:ec2 + 1], wdqb[:, 0:D], AX.X, ALU.add))
        p.op("vector", lambda e, ec2=ec, c2=cs_t: e.tensor_scalar(
            c2[:, ec2:ec2 + 1], c2[:, ec2:ec2 + 1], -1.0, None, ALU.mult))
        transpose_out(D, lambda j, g, e2=ec, d2=dstT:
                      d2.ap()[e2, j:j + g].rearrange("c p n -> p c n"), vb)

    for (w_src, dst) in ((wu_in, wqT_u), (wg_in, wqT_g)):
        for ic in range(NIC):
            dq_units.append(lambda w2=w_src, d2=dst, i2=ic: _dq_ug(w2, d2, i2))
    for hc in range(NHC):
        for ih in range(2):
            dq_units.append(lambda h2=hc, i2=ih: _dq_dn(h2, i2))
    for (w_src, dstT, cs_t) in ((wdec_in, wT_dec, csdn), (winj_in, wT_inj, csin)):
        for ec in range(NEC):
            dq_units.append(lambda w2=w_src, d2=dstT, c2=cs_t, e2=ec: _dq_ssm(w2, d2, c2, e2))

    bno = [0]

    def allreduce_maxes(m, src_col, dst_col):
        k = bno[0]
        bno[0] += 1
        tv = p.op("tensor", lambda e: e.transpose(psT[0:m, 0:128],
                                                  runm[:, src_col:src_col + m], idf[:, :]),
                  waits=[p.wv("vector"), p.wv("scalar"), p.wv("gpsimd")])
        vv = p.op("vector", lambda e: e.tensor_reduce(mxc[0:m, 0:1], psT[0:m, 0:128],
                                                      AX.X, ALU.max),
                  waits=[tv])
        d_up = p.dma_op("sync", cc_i[k].ap()[0:1, 0:m].rearrange("o m -> m o"),
                        mxc[0:m, 0:1], waits=[vv])
        cv = p.cc_op(cc_i[k].ap()[0:1, 0:m], cc_o[k].ap()[0:1, 0:m],
                     waits=[d_up])
        return p.dma_op("sync", scal[:, dst_col:dst_col + m],
                        cc_o[k].ap()[0:1, 0:m].broadcast_to([128, m]),
                        waits=[cv])

    def ln_stats_tc(tc, spill_xbf, r_dst):
        tv = None
        mmx = [0] * NDC
        mmx2 = [0] * NDC
        for dc in range(NDC):
            sl = dc % 2
            vv = p.op("vector", lambda e, d2=dc, s2=sl: e.tensor_copy(
                xbfb[:, s2, :], xs[d2][:, tc * 512:(tc + 1) * 512]),
                waits=[mmx[dc - 2] if dc >= 2 else p.wv("tensor"),
                       p.wv("scalar"),
                       p.wd("gpsimd") if spill_xbf else None])
            sv = p.op("scalar", lambda e, d2=dc, s2=sl: e.activation(
                xsqb[:, s2, :], xs[d2][:, tc * 512:(tc + 1) * 512], AF.Square),
                waits=[mmx2[dc - 2] if dc >= 2 else p.wv("tensor")])
            mmx[dc] = p.op("tensor", lambda e, s2=sl, d2=dc: e.matmul(
                psum[:, 0, :], ones[:, :], xbfb[:, s2, :],
                start=(d2 == 0), stop=(d2 == NDC - 1)),
                waits=[vv], guard=False)
            tv = mmx2[dc] = p.op("tensor", lambda e, s2=sl, d2=dc: e.matmul(
                psum[:, 1, :], ones[:, :], xsqb[:, s2, :],
                start=(d2 == 0), stop=(d2 == NDC - 1)),
                waits=[sv], guard=False)
            if spill_xbf:
                p.dma_op("gpsimd", sp_xbf[tc, dc], xbfb[:, sl, :],
                         waits=[vv])
        p.op("vector", lambda e: e.tensor_scalar(mt[:, :], psum[:, 0, :], 1.0 / D, None,
                                                 ALU.mult), waits=[tv])
        p.op("vector", lambda e: e.tensor_scalar(vt[:, :], psum[:, 1, :], 1.0 / D, None,
                                                 ALU.mult))
        p.op("vector", lambda e: e.scalar_tensor_tensor(rt[:, :], mt[:, :], -1.0, mt[:, :],
                                                        ALU.mult, ALU.mult))
        vv = p.op("vector", lambda e: e.tensor_tensor(vt[:, :], vt[:, :], rt[:, :], ALU.add))
        p.op("scalar", lambda e: e.activation(vt[:, :], vt[:, :], AF.Ln, bias=cst[:, 0:1]),
             waits=[vv])
        return p.op("scalar", lambda e, rd=r_dst: e.activation(rd, vt[:, :], AF.Exp,
                                                               scale=cst[:, 1:2]))

    def ln1_make():
        st = {"tok": [None, None]}
        p.op("vector", lambda e: e.tensor_scalar(runm[:, 0:4], runm[:, 0:4], 0.0, None,
                                                 ALU.mult))

        def unit(tc):
            sv = ln_stats_tc(tc, False, rt[:, :])
            for dc in range(NDC):
                sl = dc % 2
                p.op("vector", lambda e, d2=dc, s2=sl, t3=tc: e.tensor_tensor(
                    xlb[:, s2, :], xs[d2][:, t3 * 512:(t3 + 1) * 512], mt[:, :],
                    ALU.subtract),
                    waits=[sv, st["tok"][sl]])
                vv = p.op("vector", lambda e, s2=sl: e.tensor_tensor(
                    xlbB(s2), xlb[:, s2, :], rt[:, :], ALU.mult))
                p.op("vector", lambda e, s2=sl, d2=dc: e.tensor_reduce(
                    tmax[:, d2:d2 + 1], xlbB(s2), AX.X, ALU.max,
                    apply_absolute_value=True))
                st["tok"][sl] = p.dma_op("gpsimd", sp_xln[tc, dc], xlbB(sl),
                                         waits=[vv])
            p.op("vector", lambda e: e.tensor_reduce(mx1[:, :], tmax[:, 0:NDC], AX.X,
                                                     ALU.max))
            p.op("vector", lambda e: e.tensor_tensor(runm[:, 0:1], runm[:, 0:1],
                                                     mx1[:, :], ALU.max))

        def finish():
            vxw = p.wv("vector")
            sxw = p.wv("scalar")
            xsp = [p.dma_op("gpsimd", sp_xq[k], xs[4 + k][:, :], waits=[vxw, sxw])
                   for k in range(4)]
            return {"xsp": xsp, "g_ln": p.wd("gpsimd")}
        return unit, finish

    def emit_ffn(pre=None):
        qxf = [xs[4 + dc // 2][:, :].bitcast(BF16)[:, (dc % 2) * T:(dc % 2 + 1) * T]
               for dc in range(NDC)]
        if pre is None:
            u_ln, f_ln = ln1_make()
            for tc in range(NTC):
                u_ln(tc)
            pre = f_ln()
        xsp = pre["xsp"]
        g_ln = pre["g_ln"]
        d_sc = allreduce_maxes(1, 0, 14)
        p.op("vector", lambda e: e.tensor_scalar(scal[:, 0:1], scal[:, 14:15], 1.0 / QMAX,
                                                 1e-12, ALU.mult, ALU.max),
             waits=[d_sc])
        p.op("vector", lambda e: e.reciprocal(scal[:, 1:2], scal[:, 0:1]))
        vz = p.op("vector", lambda e: e.tensor_scalar(runm[:, 0:2], runm[:, 0:2], 0.0,
                                                      None, ALU.mult))
        ts1tok = None
        a2tok = [None, None]
        qxtok = [None] * NTC
        for tc in range(NTC):
            d_x = p.dma_op("sync", xgv[:, :, :], sp_xln[tc].rearrange("d p n -> p d n"),
                           waits=[g_ln, ts1tok] +
                                 ([p.wv("vector"), p.wv("tensor")] if tc == 0 else []))
            for dc in range(NDC):
                sl = dc % 2
                tv1 = p.op("vector", lambda e, d2=dc, s2=sl: e.tensor_scalar(
                    xlb[:, s2, :], xgv[:, d2, :], scal[:, 1:2], C_RNE, ALU.mult, ALU.add),
                    waits=[d_x, vz, a2tok[sl]])
                a2tok[sl] = p.op("scalar", lambda e, d2=dc, s2=sl, t3=tc: e.activation(
                    qxf[d2][:, t3 * 512:(t3 + 1) * 512], xlb[:, s2, :], AF.Identity,
                    bias=cst[:, 3:4]),
                    waits=[tv1, xsp[dc // 2]])
            ts1tok = tv1
            qxtok[tc] = a2tok[1]
        evtok = {0: None, 1: None, 2: None, 3: None}
        sptok = [None, None]
        mictok = [None] * NIC
        for ic in range(NIC):
            ws0, ws1 = (2 * ic) % 4, (2 * ic + 1) % 4
            d_wu = p.dma_op("sync", wls[:, ws0, :, :],
                            wqT_u[ic].rearrange("c p n -> p c n"),
                            waits=[mictok[ic - 2] if ic >= 2 else None,
                                   init_g if ic == 0 else None,
                                   p.wv("vector") if ic == 0 else None])
            d_wg = p.dma_op("sync", wls[:, ws1, :, :],
                            wqT_g[ic].rearrange("c p n -> p c n"))
            gtok = None
            for tc in range(NTC):
                for (ui, d_w, ws2, spd) in ((0, d_wu, ws0, sp_up), (1, d_wg, ws1, sp_gate)):
                    bank = 2 * (tc % 2) + ui
                    for dc in range(NDC):
                        gtok = p.op("tensor", lambda e, w2=ws2, d2=dc, t3=tc, b2=bank: e.matmul(
                            psum[:, b2, :], wls[:, w2, d2, :],
                            qxf[d2][:, t3 * 512:(t3 + 1) * 512],
                            start=(d2 == 0), stop=(d2 == NDC - 1)),
                            waits=[d_w, qxtok[tc], evtok[bank]] if dc == 0 else [],
                            guard=False, inc=(dc == NDC - 1))
                    es = (tc % 2) + 2 * ui
                    ev = p.op("vector", lambda e, b2=bank, e2=es: e.tensor_scalar(
                        evb[:, e2, :], psum[:, b2, :], 1.0, None, ALU.mult, ALU.max,
                        accum_out=tmax[:, b2:b2 + 1]),
                        waits=[gtok, sptok[ui]])
                    evtok[bank] = ev
                    p.op("vector", lambda e, u2=ui, b2=bank: e.tensor_tensor(
                        runm[:, u2:u2 + 1], runm[:, u2:u2 + 1], tmax[:, b2:b2 + 1],
                        ALU.max))
                    if tc % 2 == 1:
                        sptok[ui] = p.dma_op(
                            "gpsimd", spd.ap()[ic, tc - 1:tc + 1].rearrange("t p n -> p t n"),
                            evb[:, 2 * ui:2 * ui + 2, :], waits=[ev])
            mictok[ic] = gtok
        g_s1 = p.wd("gpsimd")
        d_sc = allreduce_maxes(2, 0, 14)
        p.op("vector", lambda e: e.tensor_tensor(scal[:, 5:6], scal[:, 14:15], scal[:, 0:1],
                                                 ALU.mult), waits=[d_sc])
        p.op("vector", lambda e: e.tensor_scalar(scal[:, 5:6], scal[:, 5:6], 1.0 / QMAX,
                                                 1e-12, ALU.mult, ALU.max))
        p.op("vector", lambda e: e.reciprocal(scal[:, 6:7], scal[:, 5:6]))
        p.op("vector", lambda e: e.tensor_tensor(scal[:, 6:7], scal[:, 6:7], scal[:, 0:1],
                                                 ALU.mult))
        p.op("vector", lambda e: e.tensor_tensor(scal[:, 2:3], scal[:, 15:16], scal[:, 0:1],
                                                 ALU.mult))
        p.op("vector", lambda e: e.tensor_scalar(scal[:, 2:3], scal[:, 2:3], 1.0 / QMAX,
                                                 1e-12, ALU.mult, ALU.max))
        p.op("vector", lambda e: e.reciprocal(scal[:, 4:5], scal[:, 2:3]))
        p.op("vector", lambda e: e.tensor_tensor(scal[:, 4:5], scal[:, 4:5], scal[:, 0:1],
                                                 ALU.mult))
        p.op("vector", lambda e: e.tensor_scalar(scal[:, 3:4], scal[:, 2:3], -C_RNE, None,
                                                 ALU.mult))
        p.op("vector", lambda e: e.tensor_scalar(scal[:, 7:8], scal[:, 2:3], 1.0 / FQ, None,
                                                 ALU.mult))
        vz = p.op("vector", lambda e: e.tensor_scalar(runm[:, 2:3], runm[:, 2:3], 0.0, None,
                                                      ALU.mult), waits=[p.wv("gpsimd")])
        NG = 2 * NIC
        kgtok = [None, None, None]
        sigtok = [None, None]
        stttok = [None, None]
        spltok = [None, None]
        dgq = [None] * NG

        def s2_load(n):
            ic, gh = n // 2, n % 2
            dgq[n] = p.dma_op("sync",
                              gld[:, n % 3, :].rearrange("p (t n) -> p t n", n=512),
                              sp_gate[ic, 4 * gh:4 * gh + 4].rearrange("t p n -> p t n"),
                              waits=[g_s1 if n < 3 else None, kgtok[n % 3], vz,
                                     p.wv("tensor") if n == 0 else None])

        def s2_post(n, sv2):
            w, s = n % 2, n % 2
            p.op("vector", lambda e, w2=w: e.tensor_scalar(
                gwk[:, w2, :], gwk[:, w2, :], C_RNE + 65535.0, -C_RNE,
                ALU.min, ALU.add),
                waits=[sv2])
            vv3 = p.op("vector", lambda e, w2=w, s2=s: e.scalar_tensor_tensor(
                spb[:, s2, :], kbf[:, w2, :], -C_RNE, gwk[:, w2, :],
                ALU.add, ALU.mult),
                waits=[spltok[s], g_s1 if n < 2 else None])
            stttok[w] = vv3
            p.op("vector", lambda e, s2=s: e.tensor_reduce(
                mx1[:, :], spb[:, s2, :], AX.X, ALU.max, apply_absolute_value=True))
            p.op("vector", lambda e: e.tensor_tensor(runm[:, 2:3], runm[:, 2:3],
                                                     mx1[:, :], ALU.max))
            ic, gh = n // 2, n % 2
            spltok[s] = p.dma_op("gpsimd",
                                 sp_P[ic, 4 * gh:4 * gh + 4].rearrange("t p n -> p t n"),
                                 spb[:, s, :].rearrange("p (t n) -> p t n", n=512),
                                 waits=[vv3])

        s2_load(0)
        s2_load(1)
        for n in range(NG):
            if n + 2 < NG:
                s2_load(n + 2)
            w = n % 2
            kgtok[n % 3] = p.op("vector", lambda e, w2=w, l2=n % 3: e.tensor_scalar(
                kbf[:, w2, :], gld[:, l2, :], scal[:, 4:5], C_RNE, ALU.mult, ALU.add),
                waits=[dgq[n], sigtok[w]])
            sigtok[w] = p.op("scalar", lambda e, w2=w: e.activation(
                gwk[:, w2, :], kbf[:, w2, :], AF.Sigmoid, scale=scal[:, 2:3],
                bias=scal[:, 3:4]),
                waits=[kgtok[n % 3], stttok[w]])
            sv2 = p.op("scalar", lambda e, w2=w: e.activation(
                gwk[:, w2, :], gwk[:, w2, :], AF.Identity, scale=cst[:, 4:5],
                bias=cst[:, 2:3]))
            if n > 0:
                s2_post(n - 1, prev_sv2)
            prev_sv2 = sv2
        s2_post(NG - 1, prev_sv2)
        g_s2 = p.wd("gpsimd")
        d_sc = allreduce_maxes(1, 2, 14)
        p.op("vector", lambda e: e.tensor_tensor(scal[:, 8:9], scal[:, 14:15], scal[:, 7:8],
                                                 ALU.mult), waits=[d_sc])
        p.op("vector", lambda e: e.tensor_scalar(scal[:, 8:9], scal[:, 8:9], 1.0 / QMAX,
                                                 1e-12, ALU.mult, ALU.max))
        p.op("vector", lambda e: e.reciprocal(scal[:, 9:10], scal[:, 8:9]))
        p.op("vector", lambda e: e.tensor_tensor(scal[:, 9:10], scal[:, 9:10], scal[:, 7:8],
                                                 ALU.mult))
        p.op("vector", lambda e: e.tensor_tensor(scal[:, 10:11], scal[:, 8:9], scal[:, 5:6],
                                                 ALU.mult))
        vz = p.op("vector", lambda e: e.tensor_scalar(runm[:, 3:4], runm[:, 3:4], 0.0, None,
                                                      ALU.mult), waits=[p.wv("gpsimd")])
        NG = 2 * NIC
        kgtok = [None, None, None]
        kutok = [None, None]
        sigtok = [None, None]
        stttok = [None, None]
        spltok = [None, None]
        dgq = [None] * NG
        duq = [None] * NG

        def s3_load(n):
            ic, gh = n // 2, n % 2
            dgq[n] = p.dma_op("sync",
                              gld[:, n % 3, :].rearrange("p (t n) -> p t n", n=512),
                              sp_P[ic, 4 * gh:4 * gh + 4].rearrange("t p n -> p t n"),
                              waits=[g_s2 if n < 3 else None, kgtok[n % 3], vz,
                                     p.wv("scalar") if n == 0 else None])

        def s3_uload(n):
            ic, gh = n // 2, n % 2
            duq[n] = p.dma_op("sync",
                              uld[:, n % 2, :].rearrange("p (t n) -> p t n", n=512),
                              sp_up[ic, 4 * gh:4 * gh + 4].rearrange("t p n -> p t n"),
                              waits=[g_s2 if n < 2 else None, kutok[n % 2]])

        def s3_post(n, sv2):
            w, s = n % 2, n % 2
            vv2 = p.op("vector", lambda e, w2=w, s2=s: e.scalar_tensor_tensor(
                spb[:, s2, :], kbf[:, w2, :], -C_RNE, gwk[:, w2, :],
                ALU.add, ALU.mult),
                waits=[sv2, spltok[s], g_s2 if n < 2 else None])
            stttok[w] = vv2
            p.op("vector", lambda e, s2=s: e.tensor_reduce(
                mx1[:, :], spb[:, s2, :], AX.X, ALU.max, apply_absolute_value=True))
            p.op("vector", lambda e: e.tensor_tensor(runm[:, 3:4], runm[:, 3:4],
                                                     mx1[:, :], ALU.max))
            ic, gh = n // 2, n % 2
            spltok[s] = p.dma_op(
                "gpsimd",
                sp_P2.ap().rearrange("t i p n -> i t p n")[ic, 4 * gh:4 * gh + 4]
                .rearrange("t p n -> p t n"),
                spb[:, s, :].rearrange("p (t n) -> p t n", n=512),
                waits=[vv2])

        s3_load(0)
        s3_load(1)
        s3_uload(0)
        for n in range(NG):
            if n + 2 < NG:
                s3_load(n + 2)
            w = n % 2
            kgtok[n % 3] = p.op("vector", lambda e, w2=w, l2=n % 3: e.tensor_scalar(
                kbf[:, w2, :], gld[:, l2, :], scal[:, 9:10], C_RNE, ALU.mult, ALU.add),
                waits=[dgq[n]])
            sv1 = p.op("scalar", lambda e, w2=w: e.activation(
                gwk[:, w2, :], uld[:, w2, :], AF.Identity, scale=scal[:, 6:7],
                bias=cst[:, 2:3]),
                waits=[duq[n], stttok[w]])
            sv2 = p.op("scalar", lambda e, w2=w: e.activation(
                gwk[:, w2, :], gwk[:, w2, :], AF.Identity, bias=cst[:, 3:4]))
            kutok[w] = sv2
            if n + 1 < NG:
                s3_uload(n + 1)
            if n > 0:
                s3_post(n - 1, prev_sv2)
            prev_sv2 = sv2
        s3_post(NG - 1, prev_sv2)
        g_s3 = p.wd("gpsimd")
        wt1 = p.wv("tensor")
        xrl = [p.dma_op("sync", xs[4 + k][:, :], sp_xq[k], waits=[wt1])
               for k in range(4)]
        d_sc = allreduce_maxes(1, 3, 14)
        p.op("vector", lambda e: e.tensor_tensor(scal[:, 11:12], scal[:, 14:15],
                                                 scal[:, 10:11], ALU.mult),
             waits=[d_sc])
        p.op("vector", lambda e: e.tensor_scalar(scal[:, 11:12], scal[:, 11:12], 1.0 / QMAX,
                                                 1e-12, ALU.mult, ALU.max))
        p.op("vector", lambda e: e.reciprocal(scal[:, 12:13], scal[:, 11:12]))
        p.op("vector", lambda e: e.tensor_tensor(scal[:, 12:13], scal[:, 12:13],
                                                 scal[:, 10:11], ALU.mult))
        vsc = p.op("vector", lambda e: e.tensor_scalar(scal[:, 13:14], scal[:, 11:12], 0.5,
                                                       None, ALU.mult))
        wv_v0 = p.wv("vector")
        wv_s0 = p.wv("scalar")
        wt_pe0 = p.wv("tensor")
        lds = [xlc[:, 0, :], xlc[:, 1, :], xbfb[:, 0, :], xbfb[:, 1, :],
               xsqb[:, 0, :], xsqb[:, 1, :]]
        ld_free = [None] * 6
        xlb_free = [None, None]
        stg_last = [None] * NTC
        gEnd = [None] * NTC
        wdhtok = [None, None, None, None]
        rstok = {2: None, 3: None, 4: None, 5: None}
        uctr = [0]

        def s4_stage(tc, ic):
            u = uctr[0]
            uctr[0] += 1
            ls = u % 6
            wsl2 = u % 2
            first = (u == 0)
            d_p2 = p.dma_op("sync", lds[ls], sp_P2[tc, ic],
                            waits=[g_s3 if first else None,
                                   wv_v0 if first else None,
                                   wv_s0 if first else None,
                                   wt_pe0 if first else None,
                                   ld_free[ls]])
            tvk = p.op("vector", lambda e, l2=ls, w2=wsl2: e.tensor_scalar(
                xlb[:, w2, :], lds[l2], scal[:, 12:13], C_RNE,
                ALU.mult, ALU.add),
                waits=[d_p2, vsc if first else None,
                       wv_s0 if first else None, xlb_free[wsl2]])
            ld_free[ls] = tvk
            sk = p.op("scalar", lambda e, t2=tc, i2=ic, w2=wsl2: e.activation(
                kob2[:, t2 % 2, i2, :], xlb[:, w2, :], AF.Identity,
                bias=cst[:, 3:4]),
                waits=[tvk, gEnd[tc - 2] if (ic == 0 and tc >= 2) else None])
            xlb_free[wsl2] = sk
            stg_last[tc] = sk

        for ic0 in range(NIC):
            s4_stage(0, ic0)
        for tc in range(NTC):
            for hc in range(NHC):
                bank = 2 + (hc % 4)
                g = None
                for ih in range(2):
                    slot = (tc * 16 + hc * 2 + ih) % 4
                    d_w = p.dma_op("gpsimd", wdh[:, slot, :, :],
                                   wqT_d[hc, 12 * ih:12 * ih + 12].rearrange("i p n -> p i n"),
                                   waits=[wdhtok[slot],
                                          g_s3 if (tc == 0 and hc < 2) else None])
                    for i12 in range(12):
                        ic = 12 * ih + i12
                        g = p.op("tensor", lambda e, b2=bank, s2=slot, i2=i12, t2=tc, ic2=ic: e.matmul(
                            psum[:, b2, :], wdh[:, s2, i2, :], kob2[:, t2 % 2, ic2, :],
                            start=(ic2 == 0), stop=(ic2 == NIC - 1)),
                            waits=([d_w, stg_last[tc], rstok[bank]] if ic == 0 else
                                   ([d_w] if i12 == 0 else [])),
                            guard=False, inc=(ic == NIC - 1))
                wdhtok[(tc * 16 + hc * 2) % 4] = wdhtok[(tc * 16 + hc * 2 + 1) % 4] = g
                rstok[bank] = p.op("vector", lambda e, h2=hc, b2=bank, t3=tc: e.scalar_tensor_tensor(
                    xs[h2][:, t3 * 512:(t3 + 1) * 512], psum[:, b2, :], scal[:, 13:14],
                    xs[h2][:, t3 * 512:(t3 + 1) * 512], ALU.mult, ALU.add),
                    waits=[g, xrl[hc - 4] if hc >= 4 else None])
                if hc == NHC - 1:
                    gEnd[tc] = g
                if tc + 1 < NTC:
                    for k in range(3):
                        s4_stage(tc + 1, 3 * hc + k)

    def emit_ssm():
        dptok = iptok = None
        for qh in range(4):
            xbftok = None
            for tq2 in range(2):
                tc = 2 * qh + tq2
                sv = ln_stats_tc(tc, True, r_q[:, tq2 * 512:(tq2 + 1) * 512])
                p.op("vector", lambda e, t2=tq2: e.tensor_tensor(
                    rm_q[:, t2 * 512:(t2 + 1) * 512], r_q[:, t2 * 512:(t2 + 1) * 512],
                    mt[:, :], ALU.mult), waits=[sv])
            v_ln = p.wv("vector")
            g_xbf = p.wd("gpsimd")
            wt0 = p.wv("tensor")
            d_x = None
            for tq2 in range(2):
                d_x = p.dma_op("sync", xkb[:, :, tq2 * 512:(tq2 + 1) * 512],
                               sp_xbf[2 * qh + tq2].rearrange("d p n -> p d n"),
                               waits=[g_xbf, wt0])
            wsltok = None
            for ec in range(NEC):
                d_wd = p.dma_op("sync", wsl[:, 0, :, :],
                                wT_dec[ec].rearrange("c p n -> p c n"),
                                waits=[wsltok, init_g if ec == 0 else None])
                d_wi = p.dma_op("sync", wsl[:, 1, :, :],
                                wT_inj[ec].rearrange("c p n -> p c n"))
                for tq2 in range(2):
                    tv0 = tv = None
                    for dc in range(NDC):
                        tv0 = p.op("tensor", lambda e, d2=dc, t2=tq2: e.matmul(
                            psum[:, 0, :], wsl[:, 0, d2, :],
                            xkb[:, d2, t2 * 512:(t2 + 1) * 512],
                            start=(d2 == 0), stop=(d2 == NDC - 1)),
                            waits=[d_x, d_wi, dptok] if dc == 0 else [],
                            guard=False, inc=(dc == NDC - 1))
                        tv = p.op("tensor", lambda e, d2=dc, t2=tq2: e.matmul(
                            psum[:, 1, :], wsl[:, 1, d2, :],
                            xkb[:, d2, t2 * 512:(t2 + 1) * 512],
                            start=(d2 == 0), stop=(d2 == NDC - 1)),
                            waits=[iptok] if dc == 0 else [],
                            guard=False, inc=(dc == NDC - 1))
                    dptok = p.op("vector", lambda e, t2=tq2: e.tensor_scalar(
                        dpre[:, t2 * 512:(t2 + 1) * 512], psum[:, 0, :], 1.0, None,
                        ALU.mult), waits=[tv0, tv])
                    iptok = p.op("vector", lambda e, t2=tq2: e.tensor_scalar(
                        ipre[:, t2 * 512:(t2 + 1) * 512], psum[:, 1, :], 1.0, None,
                        ALU.mult))
                wsltok = tv
                p.op("vector", lambda e: e.tensor_tensor(dpre[:, :], dpre[:, :], r_q[:, :],
                                                         ALU.mult),
                     waits=[v_ln])
                p.op("vector", lambda e, e2=ec: e.scalar_tensor_tensor(
                    dpre[:, :], rm_q[:, :], csdn[:, e2:e2 + 1], dpre[:, :],
                    ALU.mult, ALU.add))
                sv = p.op("scalar", lambda e, e2=ec: e.activation(
                    dpre[:, :], dpre[:, :], AF.Sigmoid, bias=bdec[:, e2:e2 + 1]),
                    waits=[p.wv("vector")])
                p.op("vector", lambda e: e.tensor_tensor(ipre[:, :], ipre[:, :], r_q[:, :],
                                                         ALU.mult))
                vv = p.op("vector", lambda e, e2=ec: e.scalar_tensor_tensor(
                    ipre[:, :], rm_q[:, :], csin[:, e2:e2 + 1], ipre[:, :],
                    ALU.mult, ALU.add))
                sv2 = p.op("scalar", lambda e, e2=ec: e.activation(
                    ipre[:, :], ipre[:, :], AF.Identity, bias=binj[:, e2:e2 + 1]),
                    waits=[vv])
                p.op("vector", lambda e, e2=ec: e.tensor_tensor_scan(
                    ipre[:, :], dpre[:, :], ipre[:, :], carr[:, e2:e2 + 1],
                    ALU.mult, ALU.add),
                    waits=[sv, sv2])
                p.op("vector", lambda e, e2=ec: e.tensor_copy(carr[:, e2:e2 + 1],
                                                              ipre[:, 1023:1024]))
                p.op("vector", lambda e, e2=ec, q2=qh: e.tensor_tensor(
                    xs[e2][:, q2 * 1024:(q2 + 1) * 1024],
                    xs[e2][:, q2 * 1024:(q2 + 1) * 1024], ipre[:, :], ALU.add))

    from collections import deque
    units = deque(dq_units)
    for k in range(T // 128):
        in_unit(k)
        if k % 2 == 1 and units:
            units.popleft()()
    pre1 = None
    if n_stages >= 1:
        u_ln, f_ln = ln1_make()
        for tc in range(NTC):
            u_ln(tc)
            for _ in range(5):
                if units:
                    units.popleft()()
        pre1 = f_ln()
    while units:
        units.popleft()()
    p.op("vector", lambda e: e.tensor_scalar(carr[:, :], carr[:, :], 0.0, None, ALU.mult))
    init_g = p.wd("gpsimd")

    if n_stages >= 1:
        emit_ffn(pre1)
    if n_stages >= 2:
        emit_ssm()
    if n_stages >= 3:
        emit_ffn()

    vfin = p.wv("vector")
    psttok = None
    odtok = [None, None]
    for tc in range(T // 128):
        sv = None
        for g4 in range(2):
            tvs = None
            for d4 in range(4):
                dc = 4 * g4 + d4
                tvs = p.op("tensor", lambda e, d2=dc, t2=tc, ps2=d4: e.transpose(
                    psT[:, ps2 * 128:(ps2 + 1) * 128],
                    xs[d2][:, t2 * 128:(t2 + 1) * 128], idf[:, :]),
                    waits=[vfin if (tc == 0 and dc == 0) else None,
                           psttok if d4 == 0 else None],
                    guard=False, inc=(d4 == 3))
            sv = psttok = p.op("scalar", lambda e, g2=g4, sl=(tc % 2): e.copy(
                obuf[:, sl, g2 * 512:(g2 + 1) * 512], psT[:, 0:512]),
                waits=[tvs, odtok[tc % 2] if g4 == 0 else None])
        odtok[tc % 2] = p.dma_op("sync", y_out[tc * 128:(tc + 1) * 128, :],
                                 obuf[:, tc % 2, :], waits=[sv])

    final_loads = p.wd("sync")
    with ExitStack() as semstack:
        sems = {}
        for nm in ("opvector", "opscalar", "optensor", "opsync", "opgpsimd", "cc"):
            sems[nm] = semstack.enter_context(nc.semaphore(nm))
        for e in ("sync", "gpsimd"):
            for i in range(Prog.NDSEM):
                key = "d" + e + str(i)
                sems[key] = semstack.enter_context(nc.semaphore(key))
        with nc.Block() as block:
            @block.sync
            def _(sync):
                for f in p.q["sync"]:
                    f(sync, sems)
                for (key, val) in final_loads:
                    sync.wait_ge(sems[key], val)

            @block.vector
            def _(vector):
                for f in p.q["vector"]:
                    f(vector, sems)

            @block.scalar
            def _(scalar):
                for f in p.q["scalar"]:
                    f(scalar, sems)

            @block.tensor
            def _(tensor):
                for f in p.q["tensor"]:
                    f(tensor, sems)

            @block.gpsimd
            def _(gpsimd):
                for f in p.q["gpsimd"]:
                    f(gpsimd, sems)
    stack.close()
    return nc


def _prep_inputs(L0_attn_x_seq, L0_attn_W_decay, L0_attn_b_decay, L0_attn_W_input,
                 L0_attn_b_input, L0_ffn_w_gate, L0_ffn_w_up, L0_ffn_w_down):
    ident_f = np.eye(128, dtype=np.float32)
    ident_b = np.eye(128, dtype=ml_dtypes.bfloat16)
    ones_b = np.ones((128, 128), dtype=ml_dtypes.bfloat16)
    bdec = np.ascontiguousarray(np.asarray(L0_attn_b_decay, np.float32).reshape(NEC, 128).T)
    binj = np.ascontiguousarray(np.asarray(L0_attn_b_input, np.float32).reshape(NEC, 128).T)
    shared = {
        "w_up": np.ascontiguousarray(np.asarray(L0_ffn_w_up, np.float32)),
        "w_gate": np.ascontiguousarray(np.asarray(L0_ffn_w_gate, np.float32)),
        "w_down": np.ascontiguousarray(np.asarray(L0_ffn_w_down, np.float32)),
        "w_dec": np.ascontiguousarray(np.asarray(L0_attn_W_decay, np.float32)),
        "w_inj": np.ascontiguousarray(np.asarray(L0_attn_W_input, np.float32)),
        "b_dec": bdec, "b_inj": binj,
        "cones": ones_b, "cidf": ident_f, "cidb": ident_b,
        "ccst": np.tile(np.array([EPS, -0.5, C_RNE, -C_RNE, FQ, 0.0, 0.0, 0.0],
                                 np.float32), (128, 1)),
    }
    x = np.asarray(L0_attn_x_seq, np.float32)
    return [dict(shared, x_in=np.ascontiguousarray(x[c])) for c in range(8)]


def kernel(**inputs):
    in_maps = _prep_inputs(**inputs)
    nc = build_program(N_STAGES)
    res = run_bass_kernel_spmd(nc, in_maps, core_ids=list(range(8)))
    return np.stack([np.asarray(res.results[c]["y"]) for c in range(8)], axis=0)


# version 09
# speedup vs baseline: 1.2295x; 1.0006x over previous
import numpy as np
import ml_dtypes
import concourse.bass as bass
import concourse.mybir as mybir
from concourse.bass_utils import run_bass_kernel_spmd

F32 = mybir.dt.float32
BF16 = mybir.dt.bfloat16
AF = mybir.ActivationFunctionType
ALU = mybir.AluOpType
AX = mybir.AxisListType

T, D, I = 4096, 1024, 3072
NDC, NIC, NHC, NEC = 8, 24, 8, 8
NTC = 8
C_RNE = 12582912.0
QMAX = 32767.0
FQ = 65536.0
EPS = 1e-5
N_STAGES = 3


class Prog:

    ENGS = ("sync", "vector", "scalar", "tensor", "gpsimd")
    NDSEM = 6

    def __init__(self):
        self.q = {e: [] for e in self.ENGS}
        self.n = {e: 0 for e in self.ENGS}
        self.di = {e: 0 for e in self.ENGS}
        self.dc = {e: [0] * self.NDSEM for e in self.ENGS}
        self.cc = 0

    @staticmethod
    def _flat(waits):
        out = []
        for w in waits:
            if w is None:
                continue
            if isinstance(w, list):
                out.extend(x for x in w if x is not None)
            else:
                out.append(w)
        return out

    def op(self, eng, f, waits=(), guard=True, inc=True):
        prev = self.n[eng] if guard else 0
        if inc:
            self.n[eng] += 1
        cnt = self.n[eng]
        ws = self._flat(waits)

        def emit(engine, sems):
            for (key, val) in ws:
                if val > 0:
                    engine.wait_ge(sems[key], val)
            if prev > 0:
                engine.wait_ge(sems["op" + eng], prev)
            ins = f(engine)
            if inc:
                ins.then_inc(sems["op" + eng], 1)
        self.q[eng].append(emit)
        return ("op" + eng, cnt)

    def dma_op(self, eng, out, in_, waits=()):
        i = self.di[eng] % self.NDSEM
        self.di[eng] += 1
        self.dc[eng][i] += 16
        cnt = self.dc[eng][i]
        key = "d" + eng + str(i)
        ws = self._flat(waits)

        def emit(engine, sems):
            for (k2, val) in ws:
                if val > 0:
                    engine.wait_ge(sems[k2], val)
            if cnt > 16:
                engine.wait_ge(sems[key], cnt - 16)
            engine.dma_start(out=out, in_=in_).then_inc(sems[key], 16)
        self.q[eng].append(emit)
        return (key, cnt)

    def cc_op(self, ins_ap, outs_ap, waits=()):
        self.cc += 1
        cnt = self.cc
        ws = self._flat(waits)

        def emit(engine, sems):
            for (key, val) in ws:
                if val > 0:
                    engine.wait_ge(sems[key], val)
            engine.collective_compute(
                "AllReduce", ALU.max, replica_groups=[list(range(8))],
                ins=[ins_ap], outs=[outs_ap]).then_inc(sems["cc"], 1)
        self.q["gpsimd"].append(emit)
        return ("cc", cnt)

    def wv(self, eng):
        return ("op" + eng, self.n[eng])

    def wd(self, eng):
        return [("d" + eng + str(i), self.dc[eng][i])
                for i in range(self.NDSEM) if self.dc[eng][i] > 0]


def build_program(n_stages=N_STAGES, debug=False):
    nc = bass.Bass("TRN2", target_bir_lowering=False, debug=False)
    p = Prog()

    x_in = nc.dram_tensor("x_in", [T, D], F32, kind="ExternalInput").ap()
    y_out = nc.dram_tensor("y", [T, D], F32, kind="ExternalOutput").ap()
    wu_in = nc.dram_tensor("w_up", [I, D], F32, kind="ExternalInput").ap()
    wg_in = nc.dram_tensor("w_gate", [I, D], F32, kind="ExternalInput").ap()
    wd_in = nc.dram_tensor("w_down", [D, I], F32, kind="ExternalInput").ap()
    wdec_in = nc.dram_tensor("w_dec", [D, D], F32, kind="ExternalInput").ap()
    winj_in = nc.dram_tensor("w_inj", [D, D], F32, kind="ExternalInput").ap()
    bdec_in = nc.dram_tensor("b_dec", [128, NEC], F32, kind="ExternalInput").ap()
    binj_in = nc.dram_tensor("b_inj", [128, NEC], F32, kind="ExternalInput").ap()
    ones_in = nc.dram_tensor("cones", [128, 128], BF16, kind="ExternalInput").ap()
    idf_in = nc.dram_tensor("cidf", [128, 128], F32, kind="ExternalInput").ap()
    idb_in = nc.dram_tensor("cidb", [128, 128], BF16, kind="ExternalInput").ap()
    cst_in = nc.dram_tensor("ccst", [128, 8], F32, kind="ExternalInput").ap()

    dram = lambda name, shape, dt=F32: nc.dram_tensor(name, shape, dt)
    ddram = (lambda name, shape, dt=F32: nc.dram_tensor(name, shape, dt,
             kind="ExternalOutput")) if debug else dram
    sp_xln = ddram("sp_xln", [NTC, NDC, 128, 512], BF16)
    sp_up = ddram("sp_up", [NIC, NTC, 128, 512], BF16)
    sp_gate = ddram("sp_gate", [NIC, NTC, 128, 512], BF16)
    sp_P = ddram("sp_P", [NIC, NTC, 128, 512], BF16)
    sp_P2 = ddram("sp_P2", [NTC, NIC, 128, 512], BF16)
    sp_xbf = dram("sp_xbf", [NTC, NDC, 128, 512], BF16)
    sp_xq = dram("sp_xq", [4, 128, T])
    wqT_u = dram("wqT_u", [NIC, NDC, 128, 128], BF16)
    wqT_g = dram("wqT_g", [NIC, NDC, 128, 128], BF16)
    wqT_d = dram("wqT_d", [NHC, NIC, 128, 128], BF16)
    wT_dec = dram("wT_dec", [NEC, NDC, 128, 128], BF16)
    wT_inj = dram("wT_inj", [NEC, NDC, 128, 128], BF16)
    dbg_out = (nc.dram_tensor("dbg", [128, 16], F32, kind="ExternalOutput").ap()
               if debug else None)
    ncc = 16
    cc_i = [dram(f"cc_i{k}", [1, 4]) for k in range(ncc)]
    cc_o = [dram(f"cc_o{k}", [1, 4]) for k in range(ncc)]

    from contextlib import ExitStack
    stack = ExitStack()
    sb = lambda name, shape, dt=F32: stack.enter_context(nc.sbuf_tensor(name, shape, dt))
    xs = [sb(f"x{dc}", [128, T]) for dc in range(NDC)]
    ones = sb("ones", [128, 128], BF16)
    idf = sb("idf", [128, 128], F32)
    idb = sb("idb", [128, 128], BF16)
    bdec = sb("bdec", [128, NEC])
    binj = sb("binj", [128, NEC])
    csdn = sb("csdn", [128, NEC])
    csin = sb("csin", [128, NEC])
    tev = sb("tev", [128, 8, 128], BF16)
    tmax = sb("tmax", [128, 8])
    runm = sb("runm", [128, 4])
    scal = sb("scal", [128, 16])
    mt = sb("mt", [128, 512])
    rt = sb("rt", [128, 512])
    vt = sb("vt", [128, 512])
    xlb = sb("xlb", [128, 2, 512])
    xlc = sb("xlc", [128, 2, 512], BF16)
    xlbB = lambda s: xlc[:, s, :]
    xbfb = sb("xbfb", [128, 2, 512], BF16)
    xsqb = sb("xsqb", [128, 2, 512], BF16)
    carr = sb("carr", [128, NEC])
    mx1 = sb("mx1", [128, 1])
    mxc = sb("mxc", [4, 1])
    cst = sb("cst", [128, 8])
    AR = 15360
    arena = sb("arena", [128, AR])

    def carve(off, n):
        assert off + n <= AR, (off, n)
        return arena[:, off:off + n], off + n

    o = 0
    stage, o = carve(o, 2048)
    stg = stage.rearrange("p (s n) -> p s n", n=1024)
    wdq, o = carve(o, 1536)
    wdqb_r, o = carve(o, 768)
    wdqb = wdqb_r.bitcast(BF16)
    srep_r, o = carve(o, 1536)
    rrep_r, o = carve(o, 1536)
    srep = srep_r.rearrange("p (b k) -> p b k", k=32)
    rrep = rrep_r.rearrange("p (b k) -> p b k", k=32)
    mxs, o = carve(o, 48)
    sss, o = carve(o, 48)
    rrr, o = carve(o, 48)

    o = 0
    xg_r, o = carve(o, 2048)
    xgv = xg_r.bitcast(BF16).rearrange("p (d n) -> p d n", n=512)
    evb_r, o = carve(o, 1024)
    evb = evb_r.bitcast(BF16).rearrange("p (s n) -> p s n", n=512)
    wls_r, o = carve(o, 2048)
    wls = wls_r.bitcast(BF16).rearrange("p (s d n) -> p s d n", s=4, d=NDC)

    o = 0
    gld_r, o = carve(o, 3072)
    gld = gld_r.bitcast(BF16).rearrange("p (s n) -> p s n", n=2048)
    gwk_r, o = carve(o, 4096)
    gwk = gwk_r.rearrange("p (s n) -> p s n", n=2048)
    kbf_r, o = carve(o, 4096)
    kbf = kbf_r.rearrange("p (s n) -> p s n", n=2048)
    uld_r, o = carve(o, 2048)
    uld = uld_r.bitcast(BF16).rearrange("p (s n) -> p s n", n=2048)
    spb_r, o = carve(o, 2048)
    spb = spb_r.bitcast(BF16).rearrange("p (s n) -> p s n", n=2048)

    o = 0
    kob_r, o = carve(o, 12288)
    kob2 = kob_r.bitcast(BF16).rearrange("p (b i n) -> p b i n", b=2, n=512)
    wdh_r, o = carve(o, 3072)
    wdh = wdh_r.bitcast(BF16).rearrange("p (s i n) -> p s i n", s=4, i=12)

    o = 0
    r_q, o = carve(o, 1024)
    rm_q, o = carve(o, 1024)
    dpre, o = carve(o, 1024)
    ipre, o = carve(o, 1024)
    xkb_r, o = carve(o, 4096)
    xkb = xkb_r.bitcast(BF16).rearrange("p (d n) -> p d n", n=1024)
    wsl_r, o = carve(o, 2048)
    wsl = wsl_r.bitcast(BF16).rearrange("p (s w d n) -> p s w d n", s=2, w=2, d=NDC)

    obuf = arena[:, 0:2048].rearrange("p (s n) -> p s n", n=1024)

    psum = stack.enter_context(nc.psum_tensor("psum", [128, 6, 512], F32))
    psT = stack.enter_context(nc.psum_tensor("psT", [128, 512], F32))
    psB = stack.enter_context(nc.psum_tensor("psB", [128, 1024], BF16))

    p.dma_op("sync", ones[:, :], ones_in)
    p.dma_op("sync", idf[:, :], idf_in)
    p.dma_op("sync", idb[:, :], idb_in)
    p.dma_op("sync", bdec[:, :], bdec_in)
    d_c2 = p.dma_op("sync", binj[:, :], binj_in)
    d_c2 = p.dma_op("sync", cst[:, :], cst_in)

    in_state = {"ev": [None, None], "ps": None}

    def in_unit(tc):
        slot = tc % 2
        d_st = p.dma_op("sync", stg[:, slot, :], x_in[tc * 128:(tc + 1) * 128, :],
                        waits=[in_state["ev"][slot], d_c2 if tc == 0 else None])
        for dc in range(NDC):
            ps = (tc * NDC + dc) % 4
            tv = p.op("tensor", lambda e, sl=slot, d2=dc, ps2=ps: e.transpose(
                psT[:, ps2 * 128:(ps2 + 1) * 128], stg[:, sl, d2 * 128:(d2 + 1) * 128],
                idf[:, :]),
                waits=[d_st, in_state["ps"]])
            in_state["ps"] = p.op("scalar", lambda e, d2=dc, t2=tc, ps2=ps: e.copy(
                xs[d2][:, t2 * 128:(t2 + 1) * 128], psT[:, ps2 * 128:(ps2 + 1) * 128]),
                waits=[tv])
        in_state["ev"][slot] = in_state["ps"]

    def dequant_tile(src_ap, ncols):
        nblk = ncols // 32
        d_w = p.dma_op("sync", wdq[:, 0:ncols], src_ap,
                       waits=[p.wv("vector"), p.wv("tensor")])
        v3 = wdq[:, 0:ncols].rearrange("p (b k) -> p b k", k=32)
        p.op("vector", lambda e: e.tensor_reduce(mxs[:, 0:nblk], v3, AX.X, ALU.max,
                                                 apply_absolute_value=True),
             waits=[d_w])
        p.op("vector", lambda e: e.tensor_scalar(sss[:, 0:nblk], mxs[:, 0:nblk],
                                                 1.0 / 7.0, 1e-12, ALU.mult, ALU.max))
        p.op("vector", lambda e: e.reciprocal(rrr[:, 0:nblk], sss[:, 0:nblk]))
        p.op("vector", lambda e: e.tensor_copy(srep[:, 0:nblk, 0:1], sss[:, 0:nblk, None]))
        p.op("vector", lambda e: e.tensor_copy(rrep[:, 0:nblk, 0:1], rrr[:, 0:nblk, None]))
        w = 1
        while w < 32:
            p.op("vector", lambda e, w2=w: e.tensor_copy(srep[:, 0:nblk, w2:2 * w2],
                                                         srep[:, 0:nblk, 0:w2]))
            p.op("vector", lambda e, w2=w: e.tensor_copy(rrep[:, 0:nblk, w2:2 * w2],
                                                         rrep[:, 0:nblk, 0:w2]))
            w *= 2
        p.op("vector", lambda e: e.tensor_tensor(v3, v3, rrep[:, 0:nblk, :], ALU.mult))
        p.op("vector", lambda e: e.tensor_scalar(wdq[:, 0:ncols], wdq[:, 0:ncols],
                                                 C_RNE, -C_RNE, ALU.add, ALU.add))
        p.op("vector", lambda e: e.tensor_tensor(v3, v3, srep[:, 0:nblk, :], ALU.mult))
        return p.op("vector", lambda e: e.tensor_copy(wdqb[:, 0:ncols], wdq[:, 0:ncols]),
                    waits=[tr_state["lastT"]])

    tr_state = {"psB": None, "tev": None, "lastT": None}

    def transpose_out(ncols, dst_gfn, v_ready):
        nch = ncols // 128
        j = 0
        while j < nch:
            g = min(8, nch - j)
            tvs = None
            for q in range(g):
                tvs = p.op("tensor", lambda e, j2=j + q, q2=q: e.transpose(
                    psB[:, q2 * 128:(q2 + 1) * 128], wdqb[:, j2 * 128:(j2 + 1) * 128],
                    idb[:, :]),
                    waits=[v_ready, tr_state["psB"]] if q == 0 else [],
                    guard=False, inc=(q == g - 1))
            sv = p.op("scalar", lambda e, g2=g: e.copy(
                tev[:, 0:g2, :], psB[:, 0:g2 * 128]),
                waits=[tvs, tr_state["tev"]])
            tr_state["psB"] = sv
            tr_state["tev"] = p.dma_op("gpsimd", dst_gfn(j, g), tev[:, 0:g, :],
                                       waits=[sv])
            j += g
        tr_state["lastT"] = tvs

    dq_units = []

    def _dq_ug(w_src, dst, ic):
        v_r = dequant_tile(w_src[ic * 128:(ic + 1) * 128, :], D)
        transpose_out(D, lambda j, g, ic2=ic, d2=dst:
                      d2.ap()[ic2, j:j + g].rearrange("c p n -> p c n"), v_r)

    def _dq_dn(hc, ih):
        v_r = dequant_tile(wd_in[hc * 128:(hc + 1) * 128, ih * 1536:(ih + 1) * 1536], 1536)
        transpose_out(1536, lambda j, g, h2=hc, i0=12 * ih:
                      wqT_d.ap()[h2, i0 + j:i0 + j + g].rearrange("c p n -> p c n"), v_r)

    def _dq_ssm(w_src, dstT, cs_t, ec):
        d_w = p.dma_op("sync", wdq[:, 0:D], w_src[ec * 128:(ec + 1) * 128, :],
                       waits=[p.wv("vector"), p.wv("tensor")])
        vb = p.op("vector", lambda e: e.tensor_copy(wdqb[:, 0:D], wdq[:, 0:D]),
                  waits=[d_w, tr_state["lastT"]])
        p.op("vector", lambda e, ec2=ec, c2=cs_t: e.tensor_reduce(
            c2[:, ec2:ec2 + 1], wdqb[:, 0:D], AX.X, ALU.add))
        p.op("vector", lambda e, ec2=ec, c2=cs_t: e.tensor_scalar(
            c2[:, ec2:ec2 + 1], c2[:, ec2:ec2 + 1], -1.0, None, ALU.mult))
        transpose_out(D, lambda j, g, e2=ec, d2=dstT:
                      d2.ap()[e2, j:j + g].rearrange("c p n -> p c n"), vb)

    for (w_src, dst) in ((wu_in, wqT_u), (wg_in, wqT_g)):
        for ic in range(NIC):
            dq_units.append(lambda w2=w_src, d2=dst, i2=ic: _dq_ug(w2, d2, i2))
    for hc in range(NHC):
        for ih in range(2):
            dq_units.append(lambda h2=hc, i2=ih: _dq_dn(h2, i2))
    for (w_src, dstT, cs_t) in ((wdec_in, wT_dec, csdn), (winj_in, wT_inj, csin)):
        for ec in range(NEC):
            dq_units.append(lambda w2=w_src, d2=dstT, c2=cs_t, e2=ec: _dq_ssm(w2, d2, c2, e2))

    bno = [0]

    def allreduce_maxes(m, src_col, dst_col):
        k = bno[0]
        bno[0] += 1
        tv = p.op("tensor", lambda e: e.transpose(psT[0:m, 0:128],
                                                  runm[:, src_col:src_col + m], idf[:, :]),
                  waits=[p.wv("vector"), p.wv("scalar"), p.wv("gpsimd")])
        vv = p.op("vector", lambda e: e.tensor_reduce(mxc[0:m, 0:1], psT[0:m, 0:128],
                                                      AX.X, ALU.max),
                  waits=[tv])
        d_up = p.dma_op("sync", cc_i[k].ap()[0:1, 0:m].rearrange("o m -> m o"),
                        mxc[0:m, 0:1], waits=[vv])
        cv = p.cc_op(cc_i[k].ap()[0:1, 0:m], cc_o[k].ap()[0:1, 0:m],
                     waits=[d_up])
        return p.dma_op("sync", scal[:, dst_col:dst_col + m],
                        cc_o[k].ap()[0:1, 0:m].broadcast_to([128, m]),
                        waits=[cv])

    def ln_stats_tc(tc, spill_xbf, r_dst):
        tv = None
        mmx = [0] * NDC
        mmx2 = [0] * NDC
        for dc in range(NDC):
            sl = dc % 2
            vv = p.op("vector", lambda e, d2=dc, s2=sl: e.tensor_copy(
                xbfb[:, s2, :], xs[d2][:, tc * 512:(tc + 1) * 512]),
                waits=[mmx[dc - 2] if dc >= 2 else p.wv("tensor"),
                       p.wv("scalar"),
                       p.wd("gpsimd") if spill_xbf else None])
            sv = p.op("scalar", lambda e, d2=dc, s2=sl: e.activation(
                xsqb[:, s2, :], xs[d2][:, tc * 512:(tc + 1) * 512], AF.Square),
                waits=[mmx2[dc - 2] if dc >= 2 else p.wv("tensor")])
            mmx[dc] = p.op("tensor", lambda e, s2=sl, d2=dc: e.matmul(
                psum[:, 0, :], ones[:, :], xbfb[:, s2, :],
                start=(d2 == 0), stop=(d2 == NDC - 1)),
                waits=[vv], guard=False)
            tv = mmx2[dc] = p.op("tensor", lambda e, s2=sl, d2=dc: e.matmul(
                psum[:, 1, :], ones[:, :], xsqb[:, s2, :],
                start=(d2 == 0), stop=(d2 == NDC - 1)),
                waits=[sv], guard=False)
            if spill_xbf:
                p.dma_op("gpsimd", sp_xbf[tc, dc], xbfb[:, sl, :],
                         waits=[vv])
        p.op("vector", lambda e: e.tensor_scalar(mt[:, :], psum[:, 0, :], 1.0 / D, None,
                                                 ALU.mult), waits=[tv])
        p.op("vector", lambda e: e.tensor_scalar(vt[:, :], psum[:, 1, :], 1.0 / D, None,
                                                 ALU.mult))
        p.op("vector", lambda e: e.scalar_tensor_tensor(rt[:, :], mt[:, :], -1.0, mt[:, :],
                                                        ALU.mult, ALU.mult))
        vv = p.op("vector", lambda e: e.tensor_tensor(vt[:, :], vt[:, :], rt[:, :], ALU.add))
        p.op("scalar", lambda e: e.activation(vt[:, :], vt[:, :], AF.Ln, bias=cst[:, 0:1]),
             waits=[vv])
        return p.op("scalar", lambda e, rd=r_dst: e.activation(rd, vt[:, :], AF.Exp,
                                                               scale=cst[:, 1:2]))

    def ln1_make():
        st = {"tok": [None, None]}
        p.op("vector", lambda e: e.tensor_scalar(runm[:, 0:4], runm[:, 0:4], 0.0, None,
                                                 ALU.mult))

        def unit(tc):
            sv = ln_stats_tc(tc, False, rt[:, :])
            for dc in range(NDC):
                sl = dc % 2
                p.op("vector", lambda e, d2=dc, s2=sl, t3=tc: e.tensor_tensor(
                    xlb[:, s2, :], xs[d2][:, t3 * 512:(t3 + 1) * 512], mt[:, :],
                    ALU.subtract),
                    waits=[sv, st["tok"][sl]])
                vv = p.op("vector", lambda e, s2=sl: e.tensor_tensor(
                    xlbB(s2), xlb[:, s2, :], rt[:, :], ALU.mult))
                p.op("vector", lambda e, s2=sl, d2=dc: e.tensor_reduce(
                    tmax[:, d2:d2 + 1], xlbB(s2), AX.X, ALU.max,
                    apply_absolute_value=True))
                st["tok"][sl] = p.dma_op("gpsimd", sp_xln[tc, dc], xlbB(sl),
                                         waits=[vv])
            p.op("vector", lambda e: e.tensor_reduce(mx1[:, :], tmax[:, 0:NDC], AX.X,
                                                     ALU.max))
            p.op("vector", lambda e: e.tensor_tensor(runm[:, 0:1], runm[:, 0:1],
                                                     mx1[:, :], ALU.max))

        def finish():
            vxw = p.wv("vector")
            sxw = p.wv("scalar")
            xsp = [p.dma_op("gpsimd", sp_xq[k], xs[4 + k][:, :], waits=[vxw, sxw])
                   for k in range(4)]
            return {"xsp": xsp, "g_ln": p.wd("gpsimd")}
        return unit, finish

    def emit_ffn(pre=None):
        qxf = [xs[4 + dc // 2][:, :].bitcast(BF16)[:, (dc % 2) * T:(dc % 2 + 1) * T]
               for dc in range(NDC)]
        if pre is None:
            u_ln, f_ln = ln1_make()
            for tc in range(NTC):
                u_ln(tc)
            pre = f_ln()
        xsp = pre["xsp"]
        g_ln = pre["g_ln"]
        d_sc = allreduce_maxes(1, 0, 14)
        p.op("vector", lambda e: e.tensor_scalar(scal[:, 0:1], scal[:, 14:15], 1.0 / QMAX,
                                                 1e-12, ALU.mult, ALU.max),
             waits=[d_sc])
        p.op("vector", lambda e: e.reciprocal(scal[:, 1:2], scal[:, 0:1]))
        vz = p.op("vector", lambda e: e.tensor_scalar(runm[:, 0:2], runm[:, 0:2], 0.0,
                                                      None, ALU.mult))
        ts1tok = None
        a2tok = [None, None]
        qxtok = [None] * NTC
        for tc in range(NTC):
            d_x = p.dma_op("sync", xgv[:, :, :], sp_xln[tc].rearrange("d p n -> p d n"),
                           waits=[g_ln, ts1tok] +
                                 ([p.wv("vector"), p.wv("tensor")] if tc == 0 else []))
            for dc in range(NDC):
                sl = dc % 2
                tv1 = p.op("vector", lambda e, d2=dc, s2=sl: e.tensor_scalar(
                    xlb[:, s2, :], xgv[:, d2, :], scal[:, 1:2], C_RNE, ALU.mult, ALU.add),
                    waits=[d_x, vz, a2tok[sl]])
                a2tok[sl] = p.op("scalar", lambda e, d2=dc, s2=sl, t3=tc: e.activation(
                    qxf[d2][:, t3 * 512:(t3 + 1) * 512], xlb[:, s2, :], AF.Identity,
                    bias=cst[:, 3:4]),
                    waits=[tv1, xsp[dc // 2]])
            ts1tok = tv1
            qxtok[tc] = a2tok[1]
        evtok = {0: None, 1: None, 2: None, 3: None}
        sptok = [None, None]
        mictok = [None] * NIC
        for ic in range(NIC):
            ws0, ws1 = (2 * ic) % 4, (2 * ic + 1) % 4
            d_wu = p.dma_op("sync", wls[:, ws0, :, :],
                            wqT_u[ic].rearrange("c p n -> p c n"),
                            waits=[mictok[ic - 2] if ic >= 2 else None,
                                   init_g if ic == 0 else None,
                                   p.wv("vector") if ic == 0 else None])
            d_wg = p.dma_op("sync", wls[:, ws1, :, :],
                            wqT_g[ic].rearrange("c p n -> p c n"))
            gtok = None
            for tc in range(NTC):
                for (ui, d_w, ws2, spd) in ((0, d_wu, ws0, sp_up), (1, d_wg, ws1, sp_gate)):
                    bank = 2 * (tc % 2) + ui
                    for dc in range(NDC):
                        gtok = p.op("tensor", lambda e, w2=ws2, d2=dc, t3=tc, b2=bank: e.matmul(
                            psum[:, b2, :], wls[:, w2, d2, :],
                            qxf[d2][:, t3 * 512:(t3 + 1) * 512],
                            start=(d2 == 0), stop=(d2 == NDC - 1)),
                            waits=[d_w, qxtok[tc], evtok[bank]] if dc == 0 else [],
                            guard=False, inc=(dc == NDC - 1))
                    es = (tc % 2) + 2 * ui
                    ev = p.op("vector", lambda e, b2=bank, e2=es: e.tensor_scalar(
                        evb[:, e2, :], psum[:, b2, :], 1.0, None, ALU.mult, ALU.max,
                        accum_out=tmax[:, b2:b2 + 1]),
                        waits=[gtok, sptok[ui]])
                    evtok[bank] = ev
                    p.op("vector", lambda e, u2=ui, b2=bank: e.tensor_tensor(
                        runm[:, u2:u2 + 1], runm[:, u2:u2 + 1], tmax[:, b2:b2 + 1],
                        ALU.max))
                    if tc % 2 == 1:
                        sptok[ui] = p.dma_op(
                            "gpsimd", spd.ap()[ic, tc - 1:tc + 1].rearrange("t p n -> p t n"),
                            evb[:, 2 * ui:2 * ui + 2, :], waits=[ev])
            mictok[ic] = gtok
        g_s1 = p.wd("gpsimd")
        d_sc = allreduce_maxes(2, 0, 14)
        p.op("vector", lambda e: e.tensor_tensor(scal[:, 5:6], scal[:, 14:15], scal[:, 0:1],
                                                 ALU.mult), waits=[d_sc])
        p.op("vector", lambda e: e.tensor_scalar(scal[:, 5:6], scal[:, 5:6], 1.0 / QMAX,
                                                 1e-12, ALU.mult, ALU.max))
        p.op("vector", lambda e: e.reciprocal(scal[:, 6:7], scal[:, 5:6]))
        p.op("vector", lambda e: e.tensor_tensor(scal[:, 6:7], scal[:, 6:7], scal[:, 0:1],
                                                 ALU.mult))
        p.op("vector", lambda e: e.tensor_tensor(scal[:, 2:3], scal[:, 15:16], scal[:, 0:1],
                                                 ALU.mult))
        p.op("vector", lambda e: e.tensor_scalar(scal[:, 2:3], scal[:, 2:3], 1.0 / QMAX,
                                                 1e-12, ALU.mult, ALU.max))
        p.op("vector", lambda e: e.reciprocal(scal[:, 4:5], scal[:, 2:3]))
        p.op("vector", lambda e: e.tensor_tensor(scal[:, 4:5], scal[:, 4:5], scal[:, 0:1],
                                                 ALU.mult))
        p.op("vector", lambda e: e.tensor_scalar(scal[:, 3:4], scal[:, 2:3], -C_RNE, None,
                                                 ALU.mult))
        p.op("vector", lambda e: e.tensor_scalar(scal[:, 7:8], scal[:, 2:3], 1.0 / FQ, None,
                                                 ALU.mult))
        vz = p.op("vector", lambda e: e.tensor_scalar(runm[:, 2:3], runm[:, 2:3], 0.0, None,
                                                      ALU.mult), waits=[p.wv("gpsimd")])
        NG = 2 * NIC
        kgtok = [None, None, None]
        sigtok = [None, None]
        stttok = [None, None]
        spltok = [None, None]
        dgq = [None] * NG

        def s2_load(n):
            ic, gh = n // 2, n % 2
            dgq[n] = p.dma_op("sync",
                              gld[:, n % 3, :].rearrange("p (t n) -> p t n", n=512),
                              sp_gate[ic, 4 * gh:4 * gh + 4].rearrange("t p n -> p t n"),
                              waits=[g_s1 if n < 3 else None, kgtok[n % 3], vz,
                                     p.wv("tensor") if n == 0 else None])

        def s2_post(n, sv2):
            w, s = n % 2, n % 2
            p.op("vector", lambda e, w2=w: e.tensor_scalar(
                gwk[:, w2, :], gwk[:, w2, :], C_RNE + 65535.0, -C_RNE,
                ALU.min, ALU.add),
                waits=[sv2])
            vv3 = p.op("vector", lambda e, w2=w, s2=s: e.scalar_tensor_tensor(
                spb[:, s2, :], kbf[:, w2, :], -C_RNE, gwk[:, w2, :],
                ALU.add, ALU.mult),
                waits=[spltok[s], g_s1 if n < 2 else None])
            stttok[w] = vv3
            p.op("vector", lambda e, s2=s: e.tensor_reduce(
                mx1[:, :], spb[:, s2, :], AX.X, ALU.max, apply_absolute_value=True))
            p.op("vector", lambda e: e.tensor_tensor(runm[:, 2:3], runm[:, 2:3],
                                                     mx1[:, :], ALU.max))
            ic, gh = n // 2, n % 2
            spltok[s] = p.dma_op("gpsimd",
                                 sp_P[ic, 4 * gh:4 * gh + 4].rearrange("t p n -> p t n"),
                                 spb[:, s, :].rearrange("p (t n) -> p t n", n=512),
                                 waits=[vv3])

        s2_load(0)
        s2_load(1)
        for n in range(NG):
            if n + 2 < NG:
                s2_load(n + 2)
            w = n % 2
            kgtok[n % 3] = p.op("vector", lambda e, w2=w, l2=n % 3: e.tensor_scalar(
                kbf[:, w2, :], gld[:, l2, :], scal[:, 4:5], C_RNE, ALU.mult, ALU.add),
                waits=[dgq[n], sigtok[w]])
            sigtok[w] = p.op("scalar", lambda e, w2=w: e.activation(
                gwk[:, w2, :], kbf[:, w2, :], AF.Sigmoid, scale=scal[:, 2:3],
                bias=scal[:, 3:4]),
                waits=[kgtok[n % 3], stttok[w]])
            sv2 = p.op("scalar", lambda e, w2=w: e.activation(
                gwk[:, w2, :], gwk[:, w2, :], AF.Identity, scale=cst[:, 4:5],
                bias=cst[:, 2:3]))
            if n > 0:
                s2_post(n - 1, prev_sv2)
            prev_sv2 = sv2
        s2_post(NG - 1, prev_sv2)
        g_s2 = p.wd("gpsimd")
        d_sc = allreduce_maxes(1, 2, 14)
        p.op("vector", lambda e: e.tensor_tensor(scal[:, 8:9], scal[:, 14:15], scal[:, 7:8],
                                                 ALU.mult), waits=[d_sc])
        p.op("vector", lambda e: e.tensor_scalar(scal[:, 8:9], scal[:, 8:9], 1.0 / QMAX,
                                                 1e-12, ALU.mult, ALU.max))
        p.op("vector", lambda e: e.reciprocal(scal[:, 9:10], scal[:, 8:9]))
        p.op("vector", lambda e: e.tensor_tensor(scal[:, 9:10], scal[:, 9:10], scal[:, 7:8],
                                                 ALU.mult))
        p.op("vector", lambda e: e.tensor_tensor(scal[:, 10:11], scal[:, 8:9], scal[:, 5:6],
                                                 ALU.mult))
        vz = p.op("vector", lambda e: e.tensor_scalar(runm[:, 3:4], runm[:, 3:4], 0.0, None,
                                                      ALU.mult), waits=[p.wv("gpsimd")])
        NG = 2 * NIC
        kgtok = [None, None, None]
        kutok = [None, None]
        sigtok = [None, None]
        stttok = [None, None]
        spltok = [None, None]
        dgq = [None] * NG
        duq = [None] * NG

        def s3_load(n):
            ic, gh = n // 2, n % 2
            dgq[n] = p.dma_op("sync",
                              gld[:, n % 3, :].rearrange("p (t n) -> p t n", n=512),
                              sp_P[ic, 4 * gh:4 * gh + 4].rearrange("t p n -> p t n"),
                              waits=[g_s2 if n < 3 else None, kgtok[n % 3], vz,
                                     p.wv("scalar") if n == 0 else None])

        def s3_uload(n):
            ic, gh = n // 2, n % 2
            duq[n] = p.dma_op("sync",
                              uld[:, n % 2, :].rearrange("p (t n) -> p t n", n=512),
                              sp_up[ic, 4 * gh:4 * gh + 4].rearrange("t p n -> p t n"),
                              waits=[g_s2 if n < 2 else None, kutok[n % 2]])

        def s3_post(n, sv2):
            w, s = n % 2, n % 2
            vv2 = p.op("vector", lambda e, w2=w, s2=s: e.scalar_tensor_tensor(
                spb[:, s2, :], kbf[:, w2, :], -C_RNE, gwk[:, w2, :],
                ALU.add, ALU.mult),
                waits=[sv2, spltok[s], g_s2 if n < 2 else None])
            stttok[w] = vv2
            p.op("vector", lambda e, s2=s: e.tensor_reduce(
                mx1[:, :], spb[:, s2, :], AX.X, ALU.max, apply_absolute_value=True))
            p.op("vector", lambda e: e.tensor_tensor(runm[:, 3:4], runm[:, 3:4],
                                                     mx1[:, :], ALU.max))
            ic, gh = n // 2, n % 2
            spltok[s] = p.dma_op(
                "gpsimd",
                sp_P2.ap().rearrange("t i p n -> i t p n")[ic, 4 * gh:4 * gh + 4]
                .rearrange("t p n -> p t n"),
                spb[:, s, :].rearrange("p (t n) -> p t n", n=512),
                waits=[vv2])

        s3_load(0)
        s3_load(1)
        s3_uload(0)
        for n in range(NG):
            if n + 2 < NG:
                s3_load(n + 2)
            w = n % 2
            kgtok[n % 3] = p.op("vector", lambda e, w2=w, l2=n % 3: e.tensor_scalar(
                kbf[:, w2, :], gld[:, l2, :], scal[:, 9:10], C_RNE, ALU.mult, ALU.add),
                waits=[dgq[n]])
            sv1 = p.op("scalar", lambda e, w2=w: e.activation(
                gwk[:, w2, :], uld[:, w2, :], AF.Identity, scale=scal[:, 6:7],
                bias=cst[:, 2:3]),
                waits=[duq[n], stttok[w]])
            sv2 = p.op("scalar", lambda e, w2=w: e.activation(
                gwk[:, w2, :], gwk[:, w2, :], AF.Identity, bias=cst[:, 3:4]))
            kutok[w] = sv2
            if n + 1 < NG:
                s3_uload(n + 1)
            if n > 0:
                s3_post(n - 1, prev_sv2)
            prev_sv2 = sv2
        s3_post(NG - 1, prev_sv2)
        g_s3 = p.wd("gpsimd")
        wt1 = p.wv("tensor")
        xrl = [p.dma_op("sync", xs[4 + k][:, :], sp_xq[k], waits=[wt1])
               for k in range(4)]
        d_sc = allreduce_maxes(1, 3, 14)
        p.op("vector", lambda e: e.tensor_tensor(scal[:, 11:12], scal[:, 14:15],
                                                 scal[:, 10:11], ALU.mult),
             waits=[d_sc])
        p.op("vector", lambda e: e.tensor_scalar(scal[:, 11:12], scal[:, 11:12], 1.0 / QMAX,
                                                 1e-12, ALU.mult, ALU.max))
        p.op("vector", lambda e: e.reciprocal(scal[:, 12:13], scal[:, 11:12]))
        p.op("vector", lambda e: e.tensor_tensor(scal[:, 12:13], scal[:, 12:13],
                                                 scal[:, 10:11], ALU.mult))
        vsc = p.op("vector", lambda e: e.tensor_scalar(scal[:, 13:14], scal[:, 11:12], 0.5,
                                                       None, ALU.mult))
        wv_v0 = p.wv("vector")
        wv_s0 = p.wv("scalar")
        wt_pe0 = p.wv("tensor")
        lds = [xlc[:, 0, :], xlc[:, 1, :], xbfb[:, 0, :], xbfb[:, 1, :],
               xsqb[:, 0, :], xsqb[:, 1, :]]
        ld_free = [None] * 6
        xlb_free = [None, None]
        stg_last = [None] * NTC
        gEnd = [None] * NTC
        wdhtok = [None, None, None, None]
        rstok = {2: None, 3: None, 4: None, 5: None}
        uctr = [0]

        def s4_stage(tc, ic):
            u = uctr[0]
            uctr[0] += 1
            ls = u % 6
            wsl2 = u % 2
            first = (u == 0)
            d_p2 = p.dma_op("sync", lds[ls], sp_P2[tc, ic],
                            waits=[g_s3 if first else None,
                                   wv_v0 if first else None,
                                   wv_s0 if first else None,
                                   wt_pe0 if first else None,
                                   ld_free[ls]])
            tvk = p.op("vector", lambda e, l2=ls, w2=wsl2: e.tensor_scalar(
                xlb[:, w2, :], lds[l2], scal[:, 12:13], C_RNE,
                ALU.mult, ALU.add),
                waits=[d_p2, vsc if first else None,
                       wv_s0 if first else None, xlb_free[wsl2]])
            ld_free[ls] = tvk
            sk = p.op("scalar", lambda e, t2=tc, i2=ic, w2=wsl2: e.activation(
                kob2[:, t2 % 2, i2, :], xlb[:, w2, :], AF.Identity,
                bias=cst[:, 3:4]),
                waits=[tvk, gEnd[tc - 2] if (ic == 0 and tc >= 2) else None])
            xlb_free[wsl2] = sk
            stg_last[tc] = sk

        for ic0 in range(NIC):
            s4_stage(0, ic0)
        for tc in range(NTC):
            for hc in range(NHC):
                bank = 2 + (hc % 4)
                g = None
                for ih in range(2):
                    slot = (tc * 16 + hc * 2 + ih) % 4
                    d_w = p.dma_op("gpsimd", wdh[:, slot, :, :],
                                   wqT_d[hc, 12 * ih:12 * ih + 12].rearrange("i p n -> p i n"),
                                   waits=[wdhtok[slot],
                                          g_s3 if (tc == 0 and hc < 2) else None])
                    for i12 in range(12):
                        ic = 12 * ih + i12
                        g = p.op("tensor", lambda e, b2=bank, s2=slot, i2=i12, t2=tc, ic2=ic: e.matmul(
                            psum[:, b2, :], wdh[:, s2, i2, :], kob2[:, t2 % 2, ic2, :],
                            start=(ic2 == 0), stop=(ic2 == NIC - 1)),
                            waits=([d_w, stg_last[tc], rstok[bank]] if ic == 0 else
                                   ([d_w] if i12 == 0 else [])),
                            guard=False, inc=(ic == NIC - 1))
                wdhtok[(tc * 16 + hc * 2) % 4] = wdhtok[(tc * 16 + hc * 2 + 1) % 4] = g
                rstok[bank] = p.op("vector", lambda e, h2=hc, b2=bank, t3=tc: e.scalar_tensor_tensor(
                    xs[h2][:, t3 * 512:(t3 + 1) * 512], psum[:, b2, :], scal[:, 13:14],
                    xs[h2][:, t3 * 512:(t3 + 1) * 512], ALU.mult, ALU.add),
                    waits=[g, xrl[hc - 4] if hc >= 4 else None])
                if hc == NHC - 1:
                    gEnd[tc] = g
                if tc + 1 < NTC:
                    for k in range(3):
                        s4_stage(tc + 1, 3 * hc + k)

    def emit_ssm():
        dptok = iptok = None
        wslt = [None, None]
        first_w = [True]

        def ssm_wload(ec):
            s = ec % 2
            p.dma_op("sync", wsl[:, s, 0, :, :],
                     wT_dec[ec].rearrange("c p n -> p c n"),
                     waits=[wslt[s], init_g if first_w[0] else None])
            first_w[0] = False
            return p.dma_op("sync", wsl[:, s, 1, :, :],
                            wT_inj[ec].rearrange("c p n -> p c n"))

        for qh in range(4):
            xbftok = None
            for tq2 in range(2):
                tc = 2 * qh + tq2
                sv = ln_stats_tc(tc, True, r_q[:, tq2 * 512:(tq2 + 1) * 512])
                p.op("vector", lambda e, t2=tq2: e.tensor_tensor(
                    rm_q[:, t2 * 512:(t2 + 1) * 512], r_q[:, t2 * 512:(t2 + 1) * 512],
                    mt[:, :], ALU.mult), waits=[sv])
            v_ln = p.wv("vector")
            g_xbf = p.wd("gpsimd")
            wt0 = p.wv("tensor")
            d_x = None
            for tq2 in range(2):
                d_x = p.dma_op("sync", xkb[:, :, tq2 * 512:(tq2 + 1) * 512],
                               sp_xbf[2 * qh + tq2].rearrange("d p n -> p d n"),
                               waits=[g_xbf, wt0])
            dcur = ssm_wload(0)
            for ec in range(NEC):
                sl = ec % 2
                for tq2 in range(2):
                    tv0 = tv = None
                    for dc in range(NDC):
                        tv0 = p.op("tensor", lambda e, s2=sl, d2=dc, t2=tq2: e.matmul(
                            psum[:, 0, :], wsl[:, s2, 0, d2, :],
                            xkb[:, d2, t2 * 512:(t2 + 1) * 512],
                            start=(d2 == 0), stop=(d2 == NDC - 1)),
                            waits=[d_x, dcur, dptok] if dc == 0 else [],
                            guard=False, inc=(dc == NDC - 1))
                        tv = p.op("tensor", lambda e, s2=sl, d2=dc, t2=tq2: e.matmul(
                            psum[:, 1, :], wsl[:, s2, 1, d2, :],
                            xkb[:, d2, t2 * 512:(t2 + 1) * 512],
                            start=(d2 == 0), stop=(d2 == NDC - 1)),
                            waits=[iptok] if dc == 0 else [],
                            guard=False, inc=(dc == NDC - 1))
                    dptok = p.op("vector", lambda e, t2=tq2: e.tensor_scalar(
                        dpre[:, t2 * 512:(t2 + 1) * 512], psum[:, 0, :], 1.0, None,
                        ALU.mult), waits=[tv0, tv])
                    iptok = p.op("vector", lambda e, t2=tq2: e.tensor_scalar(
                        ipre[:, t2 * 512:(t2 + 1) * 512], psum[:, 1, :], 1.0, None,
                        ALU.mult))
                wslt[sl] = tv
                if ec + 1 < NEC:
                    dcur = ssm_wload(ec + 1)
                p.op("vector", lambda e: e.tensor_tensor(dpre[:, :], dpre[:, :], r_q[:, :],
                                                         ALU.mult),
                     waits=[v_ln])
                p.op("vector", lambda e, e2=ec: e.scalar_tensor_tensor(
                    dpre[:, :], rm_q[:, :], csdn[:, e2:e2 + 1], dpre[:, :],
                    ALU.mult, ALU.add))
                sv = p.op("scalar", lambda e, e2=ec: e.activation(
                    dpre[:, :], dpre[:, :], AF.Sigmoid, bias=bdec[:, e2:e2 + 1]),
                    waits=[p.wv("vector")])
                p.op("vector", lambda e: e.tensor_tensor(ipre[:, :], ipre[:, :], r_q[:, :],
                                                         ALU.mult))
                vv = p.op("vector", lambda e, e2=ec: e.scalar_tensor_tensor(
                    ipre[:, :], rm_q[:, :], csin[:, e2:e2 + 1], ipre[:, :],
                    ALU.mult, ALU.add))
                sv2 = p.op("scalar", lambda e, e2=ec: e.activation(
                    ipre[:, :], ipre[:, :], AF.Identity, bias=binj[:, e2:e2 + 1]),
                    waits=[vv])
                p.op("vector", lambda e, e2=ec: e.tensor_tensor_scan(
                    ipre[:, :], dpre[:, :], ipre[:, :], carr[:, e2:e2 + 1],
                    ALU.mult, ALU.add),
                    waits=[sv, sv2])
                p.op("vector", lambda e, e2=ec: e.tensor_copy(carr[:, e2:e2 + 1],
                                                              ipre[:, 1023:1024]))
                p.op("vector", lambda e, e2=ec, q2=qh: e.tensor_tensor(
                    xs[e2][:, q2 * 1024:(q2 + 1) * 1024],
                    xs[e2][:, q2 * 1024:(q2 + 1) * 1024], ipre[:, :], ALU.add))

    from collections import deque
    units = deque(dq_units)
    for k in range(T // 128):
        in_unit(k)
        if k % 2 == 1 and units:
            units.popleft()()
    pre1 = None
    if n_stages >= 1:
        u_ln, f_ln = ln1_make()
        for tc in range(NTC):
            u_ln(tc)
            for _ in range(5):
                if units:
                    units.popleft()()
        pre1 = f_ln()
    while units:
        units.popleft()()
    p.op("vector", lambda e: e.tensor_scalar(carr[:, :], carr[:, :], 0.0, None, ALU.mult))
    init_g = p.wd("gpsimd")

    if n_stages >= 1:
        emit_ffn(pre1)
    if n_stages >= 2:
        emit_ssm()
    if n_stages >= 3:
        emit_ffn()

    vfin = p.wv("vector")
    psttok = None
    odtok = [None, None]
    for tc in range(T // 128):
        sv = None
        for g4 in range(2):
            tvs = None
            for d4 in range(4):
                dc = 4 * g4 + d4
                tvs = p.op("tensor", lambda e, d2=dc, t2=tc, ps2=d4: e.transpose(
                    psT[:, ps2 * 128:(ps2 + 1) * 128],
                    xs[d2][:, t2 * 128:(t2 + 1) * 128], idf[:, :]),
                    waits=[vfin if (tc == 0 and dc == 0) else None,
                           psttok if d4 == 0 else None],
                    guard=False, inc=(d4 == 3))
            sv = psttok = p.op("scalar", lambda e, g2=g4, sl=(tc % 2): e.copy(
                obuf[:, sl, g2 * 512:(g2 + 1) * 512], psT[:, 0:512]),
                waits=[tvs, odtok[tc % 2] if g4 == 0 else None])
        odtok[tc % 2] = p.dma_op("sync", y_out[tc * 128:(tc + 1) * 128, :],
                                 obuf[:, tc % 2, :], waits=[sv])

    final_loads = p.wd("sync")
    with ExitStack() as semstack:
        sems = {}
        for nm in ("opvector", "opscalar", "optensor", "opsync", "opgpsimd", "cc"):
            sems[nm] = semstack.enter_context(nc.semaphore(nm))
        for e in ("sync", "gpsimd"):
            for i in range(Prog.NDSEM):
                key = "d" + e + str(i)
                sems[key] = semstack.enter_context(nc.semaphore(key))
        with nc.Block() as block:
            @block.sync
            def _(sync):
                for f in p.q["sync"]:
                    f(sync, sems)
                for (key, val) in final_loads:
                    sync.wait_ge(sems[key], val)

            @block.vector
            def _(vector):
                for f in p.q["vector"]:
                    f(vector, sems)

            @block.scalar
            def _(scalar):
                for f in p.q["scalar"]:
                    f(scalar, sems)

            @block.tensor
            def _(tensor):
                for f in p.q["tensor"]:
                    f(tensor, sems)

            @block.gpsimd
            def _(gpsimd):
                for f in p.q["gpsimd"]:
                    f(gpsimd, sems)
    stack.close()
    return nc


def _prep_inputs(L0_attn_x_seq, L0_attn_W_decay, L0_attn_b_decay, L0_attn_W_input,
                 L0_attn_b_input, L0_ffn_w_gate, L0_ffn_w_up, L0_ffn_w_down):
    ident_f = np.eye(128, dtype=np.float32)
    ident_b = np.eye(128, dtype=ml_dtypes.bfloat16)
    ones_b = np.ones((128, 128), dtype=ml_dtypes.bfloat16)
    bdec = np.ascontiguousarray(np.asarray(L0_attn_b_decay, np.float32).reshape(NEC, 128).T)
    binj = np.ascontiguousarray(np.asarray(L0_attn_b_input, np.float32).reshape(NEC, 128).T)
    shared = {
        "w_up": np.ascontiguousarray(np.asarray(L0_ffn_w_up, np.float32)),
        "w_gate": np.ascontiguousarray(np.asarray(L0_ffn_w_gate, np.float32)),
        "w_down": np.ascontiguousarray(np.asarray(L0_ffn_w_down, np.float32)),
        "w_dec": np.ascontiguousarray(np.asarray(L0_attn_W_decay, np.float32)),
        "w_inj": np.ascontiguousarray(np.asarray(L0_attn_W_input, np.float32)),
        "b_dec": bdec, "b_inj": binj,
        "cones": ones_b, "cidf": ident_f, "cidb": ident_b,
        "ccst": np.tile(np.array([EPS, -0.5, C_RNE, -C_RNE, FQ, 0.0, 0.0, 0.0],
                                 np.float32), (128, 1)),
    }
    x = np.asarray(L0_attn_x_seq, np.float32)
    return [dict(shared, x_in=np.ascontiguousarray(x[c])) for c in range(8)]


def kernel(**inputs):
    in_maps = _prep_inputs(**inputs)
    nc = build_program(N_STAGES)
    res = run_bass_kernel_spmd(nc, in_maps, core_ids=list(range(8)))
    return np.stack([np.asarray(res.results[c]["y"]) for c in range(8)], axis=0)
